# version 02
# baseline (speedup 1.0000x reference)
import numpy as np
import ml_dtypes
from contextlib import ExitStack

import concourse.bass as bass
import concourse.mybir as mybir
from concourse.bass_utils import run_bass_kernel_spmd

F32 = mybir.dt.float32
BF16 = mybir.dt.bfloat16
OP = mybir.AluOpType
AF = mybir.ActivationFunctionType
AX = mybir.AxisListType

NC = 8
T = 2048
D = 2048
NH = 16
DENOM = float(np.sqrt(2048.0) / 8192.0)
EPS = 1e-5
PHASES = 2
DEBUG = False

G_OWN = [(0, 6), (6, 12), (12, 17), (17, 22), (22, 28), (28, 33), (33, 38), (38, 43)]
GPAD = 6
REG_G0 = 0
REG_U0 = GPAD * 2048
REG_D0 = 2 * GPAD * 2048
REG_COLS = REG_D0 + 2 * 5504
CHUNKS = [(0, 4096), (4096, 8192), (8192, 12288),
          (12288, 16384), (16384, 20480), (20480, 24576),
          (24576, 30080), (30080, 35584)]
NCHUNK = len(CHUNKS)
CC_VAL = {1: 1, 2: 2, "carry": 3, "fl1": 4, 3: 5, 4: 6, 5: 7, 6: 8, 7: 9, 8: 10,
          "fl2": 11}
CC_DONE = {1: 2, 2: 3, "carry": 4, 3: 6, 4: 7, 5: 8, 6: 9, 7: 10, 8: 11}

SWEEPS = [list(range(0, 15)), list(range(15, 29)), list(range(29, 43))]

def _gate_loc(ot):
    for c, (a, b) in enumerate(G_OWN):
        if a <= ot < b:
            return c, REG_G0 + (ot - a) * 2048
    raise AssertionError(ot)

def _up_loc(ot):
    c, col = _gate_loc(ot)
    return c, col + REG_U0

def _down_loc(ft):
    return ft // 2, REG_D0 + (ft % 2) * 5504

def _chunk_of_col(col):
    for j, (a, b) in enumerate(CHUNKS):
        if a <= col < b:
            return j + 1
    raise AssertionError(col)

def _mk_pieces():
    out = []
    for j, (a, b) in enumerate(CHUNKS):
        col = a
        while col < b:
            w = min(1024, b - col)
            out.append((j + 1, col, w))
            col += w
    return out

PIECES = _mk_pieces()
NPIECE = len(PIECES)
PIECE_CUM = [min(16, et + 1) for et in range(16)]
CHUNK_CLOSE_PIECE = {}
for _pi, (_j, _c, _w) in enumerate(PIECES):
    if _pi + 1 == NPIECE or PIECES[_pi + 1][0] != _j:
        CHUNK_CLOSE_PIECE[_j] = _pi

def W_XTM(t):      return 16 * (t // 2 + 1)
def W_XFT(h):      return 16 * (h // 2 + 1)
W_BC = 32
def W_K(et):       return 16 * (et // 2 + 1)
def W_Q(et):       return 16 * (8 + et // 2 + 1)
def W_V(et):       return 16 * (et // 2 + 1)
def W_WOC(ft):     return 16 * (8 + ft // 2 + 1)
def W_XRL(ft):     return 16 * (ft + 1)
def W_YRL(s, ft):  return 16 * (16 + 16 * s + ft + 1)
def W_G(ot):       return 16 * (ot // 2 + 1)
def W_U(ot):       return 16 * (ot // 2 + 1)
def W_D(s, ft):    return 16 * ((16 * s + ft) // 2 + 1)
def W_IDX(j):      return 16 * ((j - 1) // 2 + 1)
W_CCS = 16 * NC
W_MRS = 16 * 3
W_CONST = 16 * 3
def W_STCC(j):     return 16 * j
W_STS = 16
def W_YA(ft):      return 16 * (ft + 1)
def W_YD(s, ft):   return 16 * (16 + 16 * s + ft + 1)
STY_FINAL = 16 * (16 + 48)

def MM_K(et, tc):
    return 8 * et + tc + 1

def MM_V(et, tc):
    return 8 * et + 4 + tc + 1

MM_A_END = 128

def MM_Q(et, tc):
    return MM_A_END + 4 * et + tc + 1

MM_Q_END = MM_A_END + 64

def MM_WO(ft, tc):
    return MM_Q_END + 4 * ft + tc + 1

MM_WO_END = MM_Q_END + 64

def _mm_ffn_plan():
    val = MM_WO_END
    plan = {}
    for s, sw in enumerate(SWEEPS):
        for ot in sw:
            for tag in ("g1", "u1", "g2", "u2"):
                val += 1
                plan[(tag, ot)] = val
        for ft in range(16):
            for tc in range(4):
                val += 1
                plan[("d", s, ft, tc)] = val
    return plan

MM_FFN = _mm_ffn_plan()

def _ac_plan():
    val = 0
    plan = {}
    def put(k):
        nonlocal val
        val += 1
        plan[k] = val
    for t in range(16):
        put(("sq", t))
    put(("sqrt",))
    for et in range(16):
        for tc in range(4):
            put(("kev", et, tc))
    for s, sw in enumerate(SWEEPS):
        for ot in sw:
            put(("silu", ot, 1))
            put(("silu", ot, 2))
    return plan

AC = _ac_plan()

def _ve_plan():
    val = 0
    plan = {}
    def put(k):
        nonlocal val
        val += 1
        plan[k] = val
    put(("init",))
    for t in range(16):
        put(("red", t))
    put(("vare",))
    put(("stats",))
    for h in range(16):
        put(("xn", h))
    pi = 0
    for et in range(16):
        for tc in range(4):
            put(("u", et, tc))
        put(("scan", et))
        while pi < PIECE_CUM[et]:
            j = PIECES[pi][0]
            if CHUNK_CLOSE_PIECE[j] == pi:
                put(("selchunk", j))
            pi += 1
    put(("s16",))
    while pi < NPIECE:
        j = PIECES[pi][0]
        if CHUNK_CLOSE_PIECE[j] == pi:
            put(("selchunk", j))
        pi += 1
    put(("carry",))
    for et in range(16):
        for tc in range(4):
            put(("vhat", et, tc))
    for ft in range(16):
        for tc in range(4):
            put(("woev", ft, tc))
    for s, sw in enumerate(SWEEPS):
        for ot in sw:
            put(("hm", ot, 1))
            put(("hm", ot, 2))
        for ft in range(16):
            for tc in range(4):
                put(("dev", s, ft, tc))
    return plan

VE = _ve_plan()


def _build_nc(phases=PHASES):
    nc = bass.Bass("TRN2", target_bir_lowering=False, debug=False)

    xtm_h = nc.dram_tensor("xtm_h", [16, 128, D], F32, kind="ExternalInput").ap()
    xft_h = nc.dram_tensor("xft_h", [16, 128, T], F32, kind="ExternalInput").ap()
    wq_h = nc.dram_tensor("wq_h", [16, 128, D], F32, kind="ExternalInput").ap()
    wk_h = nc.dram_tensor("wk_h", [16, 128, D], F32, kind="ExternalInput").ap()
    wv_h = nc.dram_tensor("wv_h", [16, 128, D], F32, kind="ExternalInput").ap()
    wo_h = nc.dram_tensor("wo_h", [16, 128, D], F32, kind="ExternalInput").ap()
    idx_h = nc.dram_tensor("idx_h", [128, REG_COLS], BF16, kind="ExternalInput").ap()
    lutb_h = nc.dram_tensor("lutb_h", [128, 16], F32, kind="ExternalInput").ap()
    mask_h = nc.dram_tensor("mask_h", [128, 8], F32, kind="ExternalInput").ap()
    ones_h = nc.dram_tensor("ones_h", [128, T], BF16, kind="ExternalInput").ap()
    y_h = nc.dram_tensor("y_h", [16, 128, T], F32, kind="ExternalOutput").ap()
    if DEBUG:
        dbg_w = nc.dram_tensor("dbg_w", [17, 128, T], BF16, kind="ExternalOutput").ap()
        dbg_xn = nc.dram_tensor("dbg_xn", [16, 128, T], BF16, kind="ExternalOutput").ap()
        dbg_s = nc.dram_tensor("dbg_s", [128, 192], F32, kind="ExternalOutput").ap()
        dbg_s2 = nc.dram_tensor("dbg_s2", [128, 144], F32, kind="ExternalOutput").ap()

    cin, cout = [], []
    for j in range(1, NCHUNK + 1):
        a, b = CHUNKS[j - 1]
        cin.append(nc.dram_tensor(f"cc_in{j}", [128, b - a], BF16, kind="Internal").ap())
        cout.append(nc.dram_tensor(f"cc_out{j}", [NC, 128, b - a], BF16,
                                   kind="Internal", addr_space="Local").ap())
    cs_in = nc.dram_tensor("cs_in", [128, 16], F32, kind="Internal").ap()
    cs_out = nc.dram_tensor("cs_out", [NC, 128, 16], F32, kind="Internal",
                            addr_space="Local").ap()
    fl_in = nc.dram_tensor("fl_in", [1, 16], F32, kind="Internal").ap()
    fl_out = [nc.dram_tensor(f"fl_out{i}", [NC, 1, 16], F32, kind="Internal",
                             addr_space="Local").ap() for i in range(2)]
    mr_d = nc.dram_tensor("mr_d", [1, 2048], F32, kind="Internal").ap()
    rs_d = nc.dram_tensor("rs_d", [1, 2048], F32, kind="Internal").ap()

    es = ExitStack()
    def sbt(name, shape, dtype):
        return es.enter_context(nc.sbuf_tensor(name, shape, dtype))

    xn = sbt("xn", [128, NH * T], BF16)
    work = sbt("work", [128, 16 * T], BF16)
    big = sbt("big", [128, 28672], BF16)
    wkvb = sbt("wkvb", [128, 4 * D], BF16)
    sgst = sbt("sgst", [128, 2048], BF16)
    stats = sbt("stats", [128, 8 * 16], F32)
    rstd16 = sbt("rstd16", [128, 16], F32)
    s16 = sbt("s16", [128, 16], F32)
    sg = sbt("sg", [128, NC * 16], F32)
    c16 = sbt("c16", [128, 3 * 16], F32)
    maskb = sbt("maskb", [128, 8], F32)
    lutb = sbt("lutb", [128, 16], F32)
    dbgsg = sbt("dbgsg", [128, NC * 16 + 8], F32) if DEBUG else None

    def bigv(c0, cols, dtype=BF16):
        apb = big[:, c0:c0 + cols]
        return apb if dtype == BF16 else apb.bitcast(dtype)

    xtmv = [bigv(0, 4096, F32), bigv(4096, 4096, F32)]
    xftv = [bigv(8192, 4096, F32), bigv(12288, 4096, F32)]
    t1v = bigv(16384, 4096, F32)
    sqscr = t1v
    mbv = bigv(20480, 4096, F32)
    rbv = bigv(24576, 4096, F32)
    selidx = [bigv(0, 5504), bigv(5504, 5504)]
    selw = bigv(11008, 5504)
    seltmp = bigv(16512, 1024)
    selacc = [bigv(17536, 1024), bigv(18560, 1024)]
    kdsb = [bigv(19584, 2048), bigv(21632, 2048)]
    spare = bigv(23680, 2048)
    ystg = [bigv(0, 2048, F32), bigv(2048, 2048, F32)]
    xrl = [bigv(4096, 2048, F32), bigv(6144, 2048, F32)]
    gbuf = [bigv(8192, 2048), bigv(10240, 2048)]
    ubuf = [bigv(12288, 2048), bigv(14336, 2048)]
    dbuf = [bigv(16384, 1920), bigv(18304, 1920)]
    ones1 = sgst[:, 0:2048]

    banks = [es.enter_context(nc.psum_tensor(f"ps{i}", [128, 512], F32))
             for i in range(8)]

    def sem(name):
        return es.enter_context(nc.semaphore(name))
    s_xtm = [sem("s_xtm0"), sem("s_xtm1")]
    s_xft = [sem("s_xft0"), sem("s_xft1")]
    s_bc = sem("s_bc")
    s_k = [sem("s_k0"), sem("s_k1")]
    s_v = [sem("s_v0"), sem("s_v1")]
    s_xr = [sem("s_xr0"), sem("s_xr1")]
    s_g = [sem("s_g0"), sem("s_g1")]
    s_u = [sem("s_u0"), sem("s_u1")]
    s_d = [sem("s_d0"), sem("s_d1")]
    s_idx = [sem("s_idx0"), sem("s_idx1")]
    s_ccs, s_mrs, s_const = sem("s_ccs"), sem("s_mrs"), sem("s_const")
    s_stcc, s_sts = sem("s_stcc"), sem("s_sts")
    s_sty = [sem("s_sty0"), sem("s_sty1")]
    s_cc, s_mm, s_ve, s_ac = sem("s_cc"), sem("s_mm"), sem("s_ve"), sem("s_ac")
    s_fin = sem("s_fin")

    def vslot(et):
        return 15 if et == 0 else et - 2

    def wslot(i, tc=None):
        if i < 0:
            base, c0 = spare, 0
        else:
            base, c0 = work, i * T
        if tc is None:
            return base[:, c0:c0 + T]
        return base[:, c0 + tc * 512: c0 + (tc + 1) * 512]

    with es:
        block = es.enter_context(nc.Block())

        @block.vector
        def _(v):
            cnt = [0]
            def vinc(inst, key):
                cnt[0] += 1
                assert VE[key] == cnt[0], (key, VE[key], cnt[0])
                inst.then_inc(s_ve, 1)
            v.memset(c16[:, 0:16], 0.0)
            vinc(v.drain(), ("init",))
            for t in range(16):
                v.wait_ge(s_xtm[t % 2], W_XTM(t))
                vinc(v.reduce_sum(stats[:, t:t + 1], xtmv[t % 2][:, :], axis=AX.X),
                     ("red", t))
            v.wait_ge(s_ac, AC[("sq", 15)])
            v.drain()
            v.tensor_scalar(out=stats[:, 32:48], in0=stats[:, 0:16],
                            scalar1=1.0 / D, scalar2=None, op0=OP.mult)
            v.tensor_scalar(out=stats[:, 48:64], in0=stats[:, 16:32],
                            scalar1=1.0 / D, scalar2=None, op0=OP.mult)
            v.drain()
            v.tensor_tensor(out=stats[:, 64:80], in0=stats[:, 32:48],
                            in1=stats[:, 32:48], op=OP.mult)
            v.drain()
            v.scalar_tensor_tensor(out=stats[:, 80:96], in0=stats[:, 64:80],
                                   scalar=-1.0, in1=stats[:, 48:64],
                                   op0=OP.mult, op1=OP.add)
            v.drain()
            vinc(v.tensor_scalar(out=stats[:, 96:112], in0=stats[:, 80:96],
                                 scalar1=EPS, scalar2=None, op0=OP.add), ("vare",))
            v.wait_ge(s_ac, AC[("sqrt",)])
            v.reciprocal(rstd16[:, :], stats[:, 112:128])
            vinc(v.drain(), ("stats",))
            for h in range(16):
                v.wait_ge(s_xft[h % 2], W_XFT(h))
                v.wait_ge(s_bc, W_BC)
                v.drain()
                v.tensor_tensor(out=t1v[:, :], in0=xftv[h % 2][:, :], in1=mbv[:, :],
                                op=OP.subtract)
                v.drain()
                vinc(v.tensor_tensor(out=xn[:, h * T:(h + 1) * T], in0=t1v[:, :],
                                     in1=rbv[:, :], op=OP.mult), ("xn", h))
            v.drain()
            def sel_piece(pi):
                j, col, w = PIECES[pi]
                a, b = CHUNKS[j - 1]
                slot = (j - 1) % 2
                ioff = col - a
                v.wait_ge(s_idx[slot], W_IDX(j))
                if col == a and j > 1:
                    v.wait_ge(s_stcc, W_STCC(j - 1))
                for l in range(16):
                    if l == 0:
                        v.tensor_scalar(out=selacc[0][:, 0:w],
                                        in0=selidx[slot][:, ioff:ioff + w],
                                        scalar1=0.0, scalar2=lutb[:, 0:1],
                                        op0=OP.is_equal, op1=OP.mult)
                        v.drain()
                    else:
                        v.tensor_scalar(out=seltmp[:, 0:w],
                                        in0=selidx[slot][:, ioff:ioff + w],
                                        scalar1=float(l), scalar2=lutb[:, l:l + 1],
                                        op0=OP.is_equal, op1=OP.mult)
                        v.drain()
                        v.tensor_tensor(out=selacc[l % 2][:, 0:w],
                                        in0=selacc[1 - (l % 2)][:, 0:w],
                                        in1=seltmp[:, 0:w], op=OP.add)
                        v.drain()
                v.tensor_copy(selw[:, ioff:ioff + w], selacc[1][:, 0:w])
                if CHUNK_CLOSE_PIECE[j] == pi:
                    vinc(v.drain(), ("selchunk", j))
                else:
                    v.drain()
            pi_done = [0]
            for et in range(16):
                for tc in range(4):
                    v.wait_ge(s_mm, MM_V(et, tc))
                    v.wait_ge(s_ac, AC[("kev", et, tc)])
                    vinc(v.scalar_tensor_tensor(
                        out=wslot(et, tc), in0=banks[4 + tc][:, :], scalar=DENOM,
                        in1=kdsb[et % 2][:, tc * 512:(tc + 1) * 512],
                        op0=OP.mult, op1=OP.mult), ("u", et, tc))
                v.drain()
                v.wait_ge(s_const, W_CONST)
                vinc(v.tensor_tensor_scan(
                    out=wslot(et - 1), data0=wslot(et), data1=ones1[:, :],
                    initial=0.0, op0=OP.add, op1=OP.mult), ("scan", et))
                v.drain()
                v.tensor_copy(s16[:, et:et + 1], wslot(et - 1)[:, T - 1:T])
                v.drain()
                while pi_done[0] < PIECE_CUM[et]:
                    sel_piece(pi_done[0])
                    pi_done[0] += 1
            vinc(v.drain(), ("s16",))
            while pi_done[0] < NPIECE:
                sel_piece(pi_done[0])
                pi_done[0] += 1
            v.wait_ge(s_ccs, W_CCS)
            for r in range(NC):
                src = c16[:, 0:16] if r == 0 else (
                    c16[:, 16:32] if (r - 1) % 2 == 0 else c16[:, 32:48])
                dst = c16[:, 16:32] if r % 2 == 0 else c16[:, 32:48]
                v.scalar_tensor_tensor(out=dst, in0=sg[:, r * 16:(r + 1) * 16],
                                       scalar=maskb[:, r:r + 1], in1=src,
                                       op0=OP.mult, op1=OP.add)
                v.drain()
            CARRY = c16[:, 32:48]
            if DEBUG:
                v.tensor_copy(dbgsg[:, 0:NC * 16], sg[:, :])
                v.tensor_copy(dbgsg[:, NC * 16:NC * 16 + 8], maskb[:, :])
            vinc(v.drain(), ("carry",))
            for et in range(16):
                for tc in range(4):
                    v.wait_ge(s_mm, MM_Q(et, tc))
                    qb = banks[tc + 4 * (et % 2)]
                    vinc(v.scalar_tensor_tensor(
                        out=wslot(vslot(et), tc), in0=wslot(et - 1, tc),
                        scalar=CARRY[:, et:et + 1], in1=qb[:, :],
                        op0=OP.add, op1=OP.mult), ("vhat", et, tc))
                v.drain()
            for ft in range(16):
                for tc in range(4):
                    hf = tc // 2
                    v.wait_ge(s_mm, MM_WO(ft, tc))
                    v.wait_ge(s_xr[hf], W_XRL(ft))
                    if ft >= 1:
                        v.wait_ge(s_sty[hf], W_YA(ft - 1))
                    b = banks[tc + 4 * (ft % 2)]
                    vinc(v.tensor_tensor(
                        out=ystg[hf][:, (tc % 2) * 512:(tc % 2 + 1) * 512],
                        in0=b[:, :], in1=xrl[hf][:, (tc % 2) * 512:(tc % 2 + 1) * 512],
                        op=OP.add), ("woev", ft, tc))
                v.drain()
            if phases < 2:
                return
            for s, sw in enumerate(SWEEPS):
                for i, ot in enumerate(sw):
                    for st in (1, 2):
                        base = 0 if st == 1 else 4
                        v.wait_ge(s_ac, AC[("silu", ot, st)])
                        v.wait_ge(s_mm, MM_FFN[("u%d" % st, ot)])
                        for k in range(2):
                            tc = (st - 1) * 2 + k
                            inst = v.tensor_tensor(
                                out=wslot(i, tc),
                                in0=sgst[:, (st - 1) * 1024 + k * 512:
                                         (st - 1) * 1024 + (k + 1) * 512],
                                in1=banks[base + 2 + k][:, :], op=OP.mult)
                            if k == 1:
                                vinc(inst, ("hm", ot, st))
                        v.drain()
                for ft in range(16):
                    for tc in range(4):
                        hf = tc // 2
                        v.wait_ge(s_mm, MM_FFN[("d", s, ft, tc)])
                        v.wait_ge(s_xr[hf], W_YRL(s, ft))
                        if ft >= 1:
                            v.wait_ge(s_sty[hf], W_YD(s, ft - 1))
                        elif s > 0:
                            v.wait_ge(s_sty[hf], W_YD(s - 1, 15))
                        else:
                            v.wait_ge(s_sty[hf], W_YA(15))
                        b = banks[tc + 4 * (ft % 2)]
                        vinc(v.tensor_tensor(
                            out=ystg[hf][:, (tc % 2) * 512:(tc % 2 + 1) * 512],
                            in0=b[:, :],
                            in1=xrl[hf][:, (tc % 2) * 512:(tc % 2 + 1) * 512],
                            op=OP.add), ("dev", s, ft, tc))
                    v.drain()

        @block.scalar
        def _(sc):
            cnt = [0]
            def ainc(inst, key):
                cnt[0] += 1
                assert AC[key] == cnt[0], (key, AC[key], cnt[0])
                inst.then_inc(s_ac, 1)
            for t in range(16):
                sc.wait_ge(s_xtm[t % 2], W_XTM(t))
                ainc(sc.activation(sqscr[:, :], xtmv[t % 2][:, :], AF.Square,
                                   accum_out=stats[:, 16 + t:17 + t]), ("sq", t))
                sc.drain()
            sc.wait_ge(s_ve, VE[("vare",)])
            ainc(sc.activation(stats[:, 112:128], stats[:, 96:112], AF.Sqrt),
                 ("sqrt",))
            sc.wait_ge(s_ve, VE[("stats",)])
            sc.dma_start(out=mr_d.rearrange("a (p c) -> (a p) c", p=128),
                         in_=stats[:, 32:48]).then_inc(s_mrs, 16)
            sc.dma_start(out=rs_d.rearrange("a (p c) -> (a p) c", p=128),
                         in_=rstd16[:, :]).then_inc(s_mrs, 16)
            sc.dma_start(out=fl_in[:, :], in_=rstd16[0:1, 0:16]).then_inc(s_mrs, 16)
            pi_done = 0
            for et in range(16):
                for tc in range(4):
                    sc.wait_ge(s_mm, MM_K(et, tc))
                    if et >= 2:
                        sc.wait_ge(s_ve, VE[("u", et - 2, 3)])
                    ainc(sc.copy(kdsb[et % 2][:, tc * 512:(tc + 1) * 512],
                                 banks[tc][:, :]), ("kev", et, tc))
                while pi_done < PIECE_CUM[et]:
                    j = PIECES[pi_done][0]
                    if CHUNK_CLOSE_PIECE[j] == pi_done:
                        a, b = CHUNKS[j - 1]
                        sc.wait_ge(s_ve, VE[("selchunk", j)])
                        sc.dma_start(out=cin[j - 1][:, :],
                                     in_=selw[:, 0:b - a]).then_inc(s_stcc, 16)
                    pi_done += 1
            while pi_done < NPIECE:
                j = PIECES[pi_done][0]
                if CHUNK_CLOSE_PIECE[j] == pi_done:
                    a, b = CHUNKS[j - 1]
                    sc.wait_ge(s_ve, VE[("selchunk", j)])
                    sc.dma_start(out=cin[j - 1][:, :],
                                 in_=selw[:, 0:b - a]).then_inc(s_stcc, 16)
                pi_done += 1
            sc.wait_ge(s_ve, VE[("s16",)])
            sc.dma_start(out=cs_in[:, :], in_=s16[:, :]).then_inc(s_sts, 16)
            for ft in range(16):
                for hf in range(2):
                    sc.wait_ge(s_ve, VE[("woev", ft, 2 * hf + 1)])
                    sc.dma_start(out=y_h[ft, :, hf * 1024:(hf + 1) * 1024],
                                 in_=ystg[hf][:, :]).then_inc(s_sty[hf], 16)
            if phases < 2:
                sc.wait_ge(s_sty[0], W_YA(15))
                sc.wait_ge(s_sty[1], W_YA(15))
                if DEBUG:
                    sc.dma_start(out=dbg_w[0, :, :], in_=spare[:, :]).then_inc(s_fin, 16)
                    for i in range(16):
                        sc.dma_start(out=dbg_w[1 + i, :, :],
                                     in_=work[:, i * T:(i + 1) * T]).then_inc(s_fin, 16)
                    for h in range(16):
                        sc.dma_start(out=dbg_xn[h, :, :],
                                     in_=xn[:, h * T:(h + 1) * T]).then_inc(s_fin, 16)
                    sc.dma_start(out=dbg_s[:, 0:16], in_=s16[:, :]).then_inc(s_fin, 16)
                    sc.dma_start(out=dbg_s[:, 16:144], in_=sg[:, :]).then_inc(s_fin, 16)
                    sc.dma_start(out=dbg_s[:, 144:192], in_=c16[:, :]).then_inc(s_fin, 16)
                    sc.dma_start(out=dbg_s2[:, 0:128], in_=stats[:, :]).then_inc(s_fin, 16)
                    sc.dma_start(out=dbg_s2[:, 128:144], in_=rstd16[:, :]).then_inc(s_fin, 16)
                    sc.wait_ge(s_fin, 16 * 38)
                return
            for s, sw in enumerate(SWEEPS):
                for i, ot in enumerate(sw):
                    for st in (1, 2):
                        base = 0 if st == 1 else 4
                        sc.wait_ge(s_mm, MM_FFN[("g%d" % st, ot)])
                        if ot > 0:
                            sc.wait_ge(s_ve, VE[("hm", ot - 1, st)])
                        for k in range(2):
                            inst = sc.activation(
                                sgst[:, (st - 1) * 1024 + k * 512:
                                     (st - 1) * 1024 + (k + 1) * 512],
                                banks[base + k][:, :], AF.Silu)
                            if k == 1:
                                ainc(inst, ("silu", ot, st))
                        sc.drain()
                for ft in range(16):
                    for hf in range(2):
                        sc.wait_ge(s_ve, VE[("dev", s, ft, 2 * hf + 1)])
                        sc.dma_start(out=y_h[ft, :, hf * 1024:(hf + 1) * 1024],
                                     in_=ystg[hf][:, :]).then_inc(s_sty[hf], 16)
            sc.wait_ge(s_sty[0], STY_FINAL)
            sc.wait_ge(s_sty[1], STY_FINAL)

        @block.gpsimd
        def _(g):
            for t in range(16):
                if t >= 2:
                    g.wait_ge(s_ve, VE[("red", t - 2)])
                    g.wait_ge(s_ac, AC[("sq", t - 2)])
                g.dma_start(out=xtmv[t % 2][:, :],
                            in_=xtm_h[t, :, :]).then_inc(s_xtm[t % 2], 16)
            g.wait_ge(s_mrs, W_MRS)
            g.dma_start(out=mbv[:, :],
                        in_=mr_d[0:1, :].partition_broadcast(128)).then_inc(s_bc, 16)
            g.dma_start(out=rbv[:, :],
                        in_=rs_d[0:1, :].partition_broadcast(128)).then_inc(s_bc, 16)
            for h in range(16):
                if h >= 2:
                    g.wait_ge(s_ve, VE[("xn", h - 2)])
                g.dma_start(out=xftv[h % 2][:, :],
                            in_=xft_h[h, :, :]).then_inc(s_xft[h % 2], 16)
            cc_at = {6: 1, 8: 2}
            for et in range(16):
                if et >= 2:
                    g.wait_ge(s_mm, MM_K(et - 2, 3))
                g.dma_start(out=wkvb[:, (et % 2) * D:(et % 2 + 1) * D],
                            in_=wk_h[et, :, :]).then_inc(s_k[et % 2], 16)
                if et >= 2:
                    g.wait_ge(s_mm, MM_V(et - 2, 3))
                g.dma_start(out=wkvb[:, (2 + et % 2) * D:(3 + et % 2) * D],
                            in_=wv_h[et, :, :]).then_inc(s_v[et % 2], 16)
                if et in cc_at:
                    j = cc_at[et]
                    g.wait_ge(s_stcc, W_STCC(j))
                    g.collective_compute(
                        "AllGather", OP.bypass,
                        replica_groups=[list(range(NC))],
                        ins=[cin[j - 1][:, :]],
                        outs=[cout[j - 1][:, :, :]]).then_inc(s_cc, 1)
            for et in range(16):
                if et < 2:
                    g.wait_ge(s_mm, MM_K(14 + et, 3))
                else:
                    g.wait_ge(s_mm, MM_Q(et - 2, 3))
                g.dma_start(out=wkvb[:, (et % 2) * D:(et % 2 + 1) * D],
                            in_=wq_h[et, :, :]).then_inc(s_k[et % 2], 16)
                if et == 1:
                    g.wait_ge(s_sts, W_STS)
                    g.collective_compute(
                        "AllGather", OP.bypass,
                        replica_groups=[list(range(NC))],
                        ins=[cs_in[:, :]],
                        outs=[cs_out[:, :, :]]).then_inc(s_cc, 1)
                    g.wait_ge(s_mrs, W_MRS)
                    g.collective_compute(
                        "AllGather", OP.bypass,
                        replica_groups=[list(range(NC))],
                        ins=[fl_in[:, :]],
                        outs=[fl_out[0][:, :, :]]).then_inc(s_cc, 1)
            for ft in range(16):
                if ft in (0, 2, 4, 6, 8, 10):
                    j = 3 + ft // 2
                    g.wait_ge(s_stcc, W_STCC(j))
                    g.collective_compute(
                        "AllGather", OP.bypass,
                        replica_groups=[list(range(NC))],
                        ins=[cin[j - 1][:, :]],
                        outs=[cout[j - 1][:, :, :]]).then_inc(s_cc, 1)
                    if j == 8:
                        g.collective_compute(
                            "AllGather", OP.bypass,
                            replica_groups=[list(range(NC))],
                            ins=[fl_in[:, :]],
                            outs=[fl_out[1][:, :, :]]).then_inc(s_cc, 1)
                for hf in range(2):
                    if ft == 0:
                        g.wait_ge(s_stcc, W_STCC(8))
                    else:
                        g.wait_ge(s_ve, VE[("woev", ft - 1, 2 * hf + 1)])
                    g.dma_start(out=xrl[hf][:, :],
                                in_=xft_h[ft, :, hf * 1024:(hf + 1) * 1024]
                                ).then_inc(s_xr[hf], 16)
                if ft < 2:
                    g.wait_ge(s_mm, MM_V(14 + ft, 3))
                else:
                    g.wait_ge(s_mm, MM_WO(ft - 2, 3))
                g.dma_start(out=wkvb[:, (2 + ft % 2) * D:(3 + ft % 2) * D],
                            in_=wo_h[ft, :, :]).then_inc(s_v[ft % 2], 16)
            if phases < 2:
                return
            for s in range(3):
                for ft in range(16):
                    for hf in range(2):
                        if s == 0:
                            g.wait_ge(s_sty[hf], W_YA(ft))
                        else:
                            g.wait_ge(s_sty[hf], W_YD(s - 1, ft))
                        if ft >= 1:
                            g.wait_ge(s_ve, VE[("dev", s, ft - 1, 2 * hf + 1)])
                        elif s > 0:
                            g.wait_ge(s_ve, VE[("dev", s - 1, 15, 2 * hf + 1)])
                        else:
                            g.wait_ge(s_ve, VE[("woev", 15, 2 * hf + 1)])
                        g.dma_start(out=xrl[hf][:, :],
                                    in_=y_h[ft, :, hf * 1024:(hf + 1) * 1024]
                                    ).then_inc(s_xr[hf], 16)

        @block.sync
        def _(sy):
            sy.dma_start(out=lutb[:, :], in_=lutb_h[:, :]).then_inc(s_const, 16)
            sy.dma_start(out=maskb[:, :], in_=mask_h[:, :]).then_inc(s_const, 16)
            sy.dma_start(out=ones1[:, :], in_=ones_h[:, :]).then_inc(s_const, 16)
            for j in range(1, NCHUNK + 1):
                a, b = CHUNKS[j - 1]
                if j > 2:
                    sy.wait_ge(s_ve, VE[("selchunk", j - 2)])
                else:
                    sy.wait_ge(s_ve, VE[("xn", 15)])
                sy.dma_start(out=selidx[(j - 1) % 2][:, 0:b - a],
                             in_=idx_h[:, a:b]).then_inc(s_idx[(j - 1) % 2], 16)
            sy.wait_ge(s_cc, CC_DONE["carry"])
            for r in range(NC):
                sy.dma_start(out=sg[:, r * 16:(r + 1) * 16],
                             in_=cs_out[r, :, :]).then_inc(s_ccs, 16)
            if phases < 2:
                return
            for s, sw in enumerate(SWEEPS):
                for ot in sw:
                    rg, colg = _gate_loc(ot)
                    ru, colu = _up_loc(ot)
                    jg, ju = _chunk_of_col(colg), _chunk_of_col(colu)
                    ag = CHUNKS[jg - 1][0]
                    au = CHUNKS[ju - 1][0]
                    sy.wait_ge(s_cc, CC_DONE[jg])
                    if ot < 2:
                        sy.wait_ge(s_stcc, W_STCC(8))
                    else:
                        sy.wait_ge(s_mm, MM_FFN[("g2", ot - 2)])
                    sy.dma_start(out=gbuf[ot % 2][:, :],
                                 in_=cout[jg - 1][rg, :, colg - ag:colg - ag + 2048]
                                 ).then_inc(s_g[ot % 2], 16)
                    sy.wait_ge(s_cc, CC_DONE[ju])
                    if ot < 2:
                        sy.wait_ge(s_stcc, W_STCC(8))
                    else:
                        sy.wait_ge(s_mm, MM_FFN[("u2", ot - 2)])
                    sy.dma_start(out=ubuf[ot % 2][:, :],
                                 in_=cout[ju - 1][ru, :, colu - au:colu - au + 2048]
                                 ).then_inc(s_u[ot % 2], 16)
                nsw = len(sw)
                it0 = sw[0]
                for ft in range(16):
                    rd, cold = _down_loc(ft)
                    jd = _chunk_of_col(cold)
                    ad = CHUNKS[jd - 1][0]
                    sy.wait_ge(s_cc, CC_DONE[jd])
                    if s == 0 and ft < 2:
                        sy.wait_ge(s_ve, VE[("u", 15, 3)])
                        sy.wait_ge(s_stcc, W_STCC(8))
                    if ft >= 2:
                        sy.wait_ge(s_mm, MM_FFN[("d", s, ft - 2, 3)])
                    elif s > 0:
                        sy.wait_ge(s_mm, MM_FFN[("d", s - 1, 14 + ft, 3)])
                    off = cold - ad + it0 * 128
                    sy.dma_start(out=dbuf[ft % 2][:, 0:nsw * 128],
                                 in_=cout[jd - 1][rd, :, off:off + nsw * 128]
                                 ).then_inc(s_d[ft % 2], 16)

        @block.tensor
        def _(te):
            cnt = [0]
            def minc(inst, val):
                cnt[0] += 1
                assert val == cnt[0], (val, cnt[0])
                inst.then_inc(s_mm, 1)
            te.wait_ge(s_ve, VE[("xn", 15)])
            for et in range(16):
                te.wait_ge(s_k[et % 2], W_K(et))
                if et >= 1:
                    for tc in range(4):
                        te.wait_ge(s_ac, AC[("kev", et - 1, tc)])
                last = {}
                for dd in range(16):
                    for tc in range(4):
                        last[tc] = te.matmul(
                            banks[tc][:, :],
                            wkvb[:, (et % 2) * D + dd * 128:(et % 2) * D + (dd + 1) * 128],
                            xn[:, dd * T + tc * 512:dd * T + (tc + 1) * 512],
                            start=(dd == 0), stop=(dd == 15))
                for tc in range(4):
                    minc(last[tc], MM_K(et, tc))
                te.wait_ge(s_v[et % 2], W_V(et))
                if et >= 1:
                    for tc in range(4):
                        te.wait_ge(s_ve, VE[("u", et - 1, tc)])
                for dd in range(16):
                    for tc in range(4):
                        last[tc] = te.matmul(
                            banks[4 + tc][:, :],
                            wkvb[:, (2 + et % 2) * D + dd * 128:(2 + et % 2) * D + (dd + 1) * 128],
                            xn[:, dd * T + tc * 512:dd * T + (tc + 1) * 512],
                            start=(dd == 0), stop=(dd == 15))
                for tc in range(4):
                    minc(last[tc], MM_V(et, tc))
            for et in range(16):
                te.wait_ge(s_k[et % 2], W_Q(et))
                if et == 0:
                    for tc in range(4):
                        te.wait_ge(s_ac, AC[("kev", 15, tc)])
                elif et == 1:
                    for tc in range(4):
                        te.wait_ge(s_ve, VE[("u", 15, tc)])
                else:
                    for tc in range(4):
                        te.wait_ge(s_ve, VE[("vhat", et - 2, tc)])
                last = {}
                for dd in range(16):
                    for tc in range(4):
                        last[tc] = te.matmul(
                            banks[tc + 4 * (et % 2)][:, :],
                            wkvb[:, (et % 2) * D + dd * 128:(et % 2) * D + (dd + 1) * 128],
                            xn[:, dd * T + tc * 512:dd * T + (tc + 1) * 512],
                            start=(dd == 0), stop=(dd == 15))
                for tc in range(4):
                    minc(last[tc], MM_Q(et, tc))
            te.wait_ge(s_ve, VE[("vhat", 15, 3)])
            for ft in range(16):
                te.wait_ge(s_v[ft % 2], W_WOC(ft))
                if ft < 2:
                    for tc in range(4):
                        te.wait_ge(s_ve, VE[("vhat", 14 + ft, tc)])
                else:
                    for tc in range(4):
                        te.wait_ge(s_ve, VE[("woev", ft - 2, tc)])
                last = {}
                for ee in range(16):
                    for tc in range(4):
                        last[tc] = te.matmul(
                            banks[tc + 4 * (ft % 2)][:, :],
                            wkvb[:, (2 + ft % 2) * D + ee * 128:(2 + ft % 2) * D + (ee + 1) * 128],
                            wslot(vslot(ee), tc),
                            start=(ee == 0), stop=(ee == 15))
                for tc in range(4):
                    minc(last[tc], MM_WO(ft, tc))
            if phases < 2:
                return
            for s, sw in enumerate(SWEEPS):
                for i, ot in enumerate(sw):
                    te.wait_ge(s_g[ot % 2], W_G(ot))
                    te.wait_ge(s_u[ot % 2], W_U(ot))
                    if s == 0 and i == 0:
                        te.wait_ge(s_ve, VE[("woev", 15, 3)])
                    elif i == 0:
                        te.wait_ge(s_ve, VE[("dev", s - 1, 15, 3)])
                    else:
                        te.wait_ge(s_ac, AC[("silu", ot - 1, 1)])
                        te.wait_ge(s_ve, VE[("hm", ot - 1, 1)])
                    lastg, lastu = {}, {}
                    for dd in range(16):
                        for k in range(2):
                            lastg[k] = te.matmul(
                                banks[k][:, :],
                                gbuf[ot % 2][:, dd * 128:(dd + 1) * 128],
                                xn[:, dd * T + k * 512:dd * T + (k + 1) * 512],
                                start=(dd == 0), stop=(dd == 15))
                        for k in range(2):
                            lastu[k] = te.matmul(
                                banks[2 + k][:, :],
                                ubuf[ot % 2][:, dd * 128:(dd + 1) * 128],
                                xn[:, dd * T + k * 512:dd * T + (k + 1) * 512],
                                start=(dd == 0), stop=(dd == 15))
                    minc(lastg[1], MM_FFN[("g1", ot)])
                    minc(lastu[1], MM_FFN[("u1", ot)])
                    if i > 0:
                        te.wait_ge(s_ac, AC[("silu", ot - 1, 2)])
                        te.wait_ge(s_ve, VE[("hm", ot - 1, 2)])
                    for dd in range(16):
                        for k in range(2):
                            lastg[k] = te.matmul(
                                banks[4 + k][:, :],
                                gbuf[ot % 2][:, dd * 128:(dd + 1) * 128],
                                xn[:, dd * T + (2 + k) * 512:dd * T + (3 + k) * 512],
                                start=(dd == 0), stop=(dd == 15))
                        for k in range(2):
                            lastu[k] = te.matmul(
                                banks[6 + k][:, :],
                                ubuf[ot % 2][:, dd * 128:(dd + 1) * 128],
                                xn[:, dd * T + (2 + k) * 512:dd * T + (3 + k) * 512],
                                start=(dd == 0), stop=(dd == 15))
                    minc(lastg[1], MM_FFN[("g2", ot)])
                    minc(lastu[1], MM_FFN[("u2", ot)])
                nsw = len(sw)
                te.wait_ge(s_ve, VE[("hm", sw[-1], 2)])
                for ft in range(16):
                    te.wait_ge(s_d[ft % 2], W_D(s, ft))
                    if ft == 0:
                        te.wait_ge(s_ac, AC[("silu", sw[-1], 1)])
                    elif ft == 1:
                        te.wait_ge(s_ac, AC[("silu", sw[-1], 2)])
                    else:
                        for tc in range(4):
                            te.wait_ge(s_ve, VE[("dev", s, ft - 2, tc)])
                    last = {}
                    for iti in range(nsw):
                        for tc in range(4):
                            last[tc] = te.matmul(
                                banks[tc + 4 * (ft % 2)][:, :],
                                dbuf[ft % 2][:, iti * 128:(iti + 1) * 128],
                                wslot(iti, tc),
                                start=(iti == 0), stop=(iti == nsw - 1))
                    for tc in range(4):
                        minc(last[tc], MM_FFN[("d", s, ft, tc)])

    return nc


def _blk(W):
    return np.ascontiguousarray(
        W.reshape(16, 128, 16, 128).transpose(0, 3, 2, 1).reshape(16, 128, D))


def kernel(L0_attn_x, L0_attn_Wq, L0_attn_Wk, L0_attn_Wv, L0_attn_Wo,
           L0_ffn_lut, L0_ffn_gate_idx, L0_ffn_up_idx, L0_ffn_down_idx):
    x = np.asarray(L0_attn_x, dtype=np.float32)
    Wq = np.asarray(L0_attn_Wq, dtype=np.float32)
    Wk = np.asarray(L0_attn_Wk, dtype=np.float32)
    Wv = np.asarray(L0_attn_Wv, dtype=np.float32)
    Wo = np.asarray(L0_attn_Wo, dtype=np.float32)
    lut = np.asarray(L0_ffn_lut, dtype=np.float32)
    gidx = np.asarray(L0_ffn_gate_idx)
    uidx = np.asarray(L0_ffn_up_idx)
    didx = np.asarray(L0_ffn_down_idx)

    wqb, wkb, wvb, wob = _blk(Wq), _blk(Wk), _blk(Wv), _blk(Wo)
    lutb = np.ascontiguousarray(np.broadcast_to(lut[None, :], (128, 16)))
    ones1 = np.ones((128, T), dtype=ml_dtypes.bfloat16)

    g_can = gidx.reshape(43, 128, 16, 128).transpose(0, 3, 2, 1).reshape(43, 128, 2048)
    u_can = uidx.reshape(43, 128, 16, 128).transpose(0, 3, 2, 1).reshape(43, 128, 2048)
    d_can = didx.reshape(16, 128, 43, 128).transpose(0, 3, 2, 1).reshape(16, 128, 5504)

    in_maps = []
    for c in range(NC):
        bb, jj = c // 4, c % 4
        xs = x[bb, jj * T:(jj + 1) * T, :]
        xtm = np.ascontiguousarray(xs.reshape(128, 16, D).transpose(1, 0, 2))
        xft = np.ascontiguousarray(xs.T.reshape(16, 128, T))
        reg = np.zeros((128, REG_COLS), np.float32)
        a0, b0 = G_OWN[c]
        for s, ot in enumerate(range(a0, b0)):
            reg[:, REG_G0 + s * 2048:REG_G0 + (s + 1) * 2048] = g_can[ot]
            reg[:, REG_U0 + s * 2048:REG_U0 + (s + 1) * 2048] = u_can[ot]
        for s, ft in enumerate((2 * c, 2 * c + 1)):
            reg[:, REG_D0 + s * 5504:REG_D0 + (s + 1) * 5504] = d_can[ft]
        regb = reg.astype(ml_dtypes.bfloat16)
        mask = np.zeros((128, 8), np.float32)
        for r in range(NC):
            if r // 4 == bb and (r % 4) < jj:
                mask[:, r] = 1.0
        in_maps.append({
            "xtm_h": xtm, "xft_h": xft,
            "wq_h": wqb, "wk_h": wkb, "wv_h": wvb, "wo_h": wob,
            "idx_h": regb, "lutb_h": lutb, "mask_h": mask, "ones_h": ones1,
        })

    nc = _build_nc()
    res = run_bass_kernel_spmd(nc, in_maps, core_ids=list(range(NC)))
    out = np.empty((2, 8192, D), np.float32)
    for c in range(NC):
        bb, jj = c // 4, c % 4
        yf = np.asarray(res.results[c]["y_h"]).reshape(D, T)
        out[bb, jj * T:(jj + 1) * T, :] = yf.T
    return out


# version 03
# speedup vs baseline: 1.0170x; 1.0170x over previous
import numpy as np
import ml_dtypes
from contextlib import ExitStack

import concourse.bass as bass
import concourse.mybir as mybir
from concourse.bass_utils import run_bass_kernel_spmd

F32 = mybir.dt.float32
BF16 = mybir.dt.bfloat16
OP = mybir.AluOpType
AF = mybir.ActivationFunctionType
AX = mybir.AxisListType

NC = 8
T = 2048
D = 2048
NH = 16
DENOM = float(np.sqrt(2048.0) / 8192.0)
EPS = 1e-5
PHASES = 2
DEBUG = False

G_OWN = [(0, 6), (6, 12), (12, 17), (17, 22), (22, 28), (28, 33), (33, 38), (38, 43)]
GPAD = 6
REG_G0 = 0
REG_U0 = GPAD * 2048
REG_D0 = 2 * GPAD * 2048
REG_COLS = REG_D0 + 2 * 5504
CHUNKS = [(0, 4096), (4096, 8192), (8192, 12288),
          (12288, 16384), (16384, 20480), (20480, 24576),
          (24576, 30080), (30080, 35584)]
NCHUNK = len(CHUNKS)
CC_VAL = {1: 1, 2: 2, "carry": 3, "fl1": 4, 3: 5, 4: 6, 5: 7, 6: 8, 7: 9, 8: 10,
          "fl2": 11}
CC_DONE = {1: 2, 2: 3, "carry": 4, 3: 6, 4: 7, 5: 8, 6: 9, 7: 10, 8: 11}

SWEEPS = [list(range(0, 15)), list(range(15, 29)), list(range(29, 43))]

def _gate_loc(ot):
    for c, (a, b) in enumerate(G_OWN):
        if a <= ot < b:
            return c, REG_G0 + (ot - a) * 2048
    raise AssertionError(ot)

def _up_loc(ot):
    c, col = _gate_loc(ot)
    return c, col + REG_U0

def _down_loc(ft):
    return ft // 2, REG_D0 + (ft % 2) * 5504

def _chunk_of_col(col):
    for j, (a, b) in enumerate(CHUNKS):
        if a <= col < b:
            return j + 1
    raise AssertionError(col)

def _mk_pieces():
    out = []
    for j, (a, b) in enumerate(CHUNKS):
        col = a
        while col < b:
            w = min(1024, b - col)
            out.append((j + 1, col, w))
            col += w
    return out

PIECES = _mk_pieces()
NPIECE = len(PIECES)
PIECE_CUM = [min(16, et + 1) for et in range(16)]
CHUNK_CLOSE_PIECE = {}
for _pi, (_j, _c, _w) in enumerate(PIECES):
    if _pi + 1 == NPIECE or PIECES[_pi + 1][0] != _j:
        CHUNK_CLOSE_PIECE[_j] = _pi

def W_XTM(t):      return 16 * (t // 2 + 1)
def W_XFT(h):      return 16 * (h // 2 + 1)
W_BC = 32
def W_K(et):       return 16 * (et // 2 + 1)
def W_Q(et):       return 16 * (8 + et // 2 + 1)
def W_V(et):       return 16 * (et // 2 + 1)
def W_WOC(ft):     return 16 * (8 + ft // 2 + 1)
def W_XRL(ft):     return 16 * (ft + 1)
def W_YRL(s, ft):  return 16 * (16 + 16 * s + ft + 1)
def W_G(ot):       return 16 * (ot // 2 + 1)
def W_U(ot):       return 16 * (ot // 2 + 1)
def W_D(s, ft):    return 16 * ((16 * s + ft) // 2 + 1)
def W_IDX(j):      return 16 * ((j - 1) // 2 + 1)
W_CCS = 16 * NC
W_MRS = 16 * 3
W_CONST = 16 * 3
def W_STCC(j):     return 16 * j
W_STS = 16
def W_YA(ft):      return 16 * (ft + 1)
def W_YD(s, ft):   return 16 * (16 + 16 * s + ft + 1)
STY_FINAL = 16 * (16 + 48)

def MM_K(et, tc):
    return 8 * et + tc + 1

def MM_V(et, tc):
    return 8 * et + 4 + tc + 1

MM_A_END = 128

def MM_Q(et, tc):
    return MM_A_END + 4 * et + tc + 1

MM_Q_END = MM_A_END + 64

def MM_WO(ft, tc):
    return MM_Q_END + 4 * ft + tc + 1

MM_WO_END = MM_Q_END + 64

def _mm_ffn_plan():
    val = MM_WO_END
    plan = {}
    for s, sw in enumerate(SWEEPS):
        for ot in sw:
            for tag in ("g1", "u1", "g2", "u2"):
                val += 1
                plan[(tag, ot)] = val
        for ft in range(16):
            for tc in range(4):
                val += 1
                plan[("d", s, ft, tc)] = val
    return plan

MM_FFN = _mm_ffn_plan()

def _ac_plan():
    val = 0
    plan = {}
    def put(k):
        nonlocal val
        val += 1
        plan[k] = val
    for t in range(16):
        put(("sq", t))
    put(("sqrt",))
    for et in range(16):
        for tc in range(4):
            put(("kev", et, tc))
    for s, sw in enumerate(SWEEPS):
        for ot in sw:
            put(("silu", ot, 1))
            put(("silu", ot, 2))
    return plan

AC = _ac_plan()

def _ve_plan():
    val = 0
    plan = {}
    def put(k):
        nonlocal val
        val += 1
        plan[k] = val
    put(("init",))
    for t in range(16):
        put(("red", t))
    put(("vare",))
    put(("stats",))
    for h in range(16):
        put(("xn", h))
    pi = 0
    for et in range(16):
        for tc in range(4):
            put(("u", et, tc))
        put(("scan", et))
        while pi < PIECE_CUM[et]:
            j = PIECES[pi][0]
            if CHUNK_CLOSE_PIECE[j] == pi:
                put(("selchunk", j))
            pi += 1
    put(("s16",))
    while pi < NPIECE:
        j = PIECES[pi][0]
        if CHUNK_CLOSE_PIECE[j] == pi:
            put(("selchunk", j))
        pi += 1
    put(("carry",))
    for et in range(16):
        for tc in range(4):
            put(("vhat", et, tc))
    for ft in range(16):
        for tc in range(4):
            put(("woev", ft, tc))
    for s, sw in enumerate(SWEEPS):
        for ot in sw:
            put(("hm", ot, 1))
            put(("hm", ot, 2))
        for ft in range(16):
            for tc in range(4):
                put(("dev", s, ft, tc))
    return plan

VE = _ve_plan()


def _build_nc(phases=PHASES):
    nc = bass.Bass("TRN2", target_bir_lowering=False, debug=False)

    xtm_h = nc.dram_tensor("xtm_h", [16, 128, D], F32, kind="ExternalInput").ap()
    xft_h = nc.dram_tensor("xft_h", [16, 128, T], F32, kind="ExternalInput").ap()
    wq_h = nc.dram_tensor("wq_h", [16, 128, D], F32, kind="ExternalInput").ap()
    wk_h = nc.dram_tensor("wk_h", [16, 128, D], F32, kind="ExternalInput").ap()
    wv_h = nc.dram_tensor("wv_h", [16, 128, D], F32, kind="ExternalInput").ap()
    wo_h = nc.dram_tensor("wo_h", [16, 128, D], F32, kind="ExternalInput").ap()
    idx_h = nc.dram_tensor("idx_h", [128, REG_COLS], BF16, kind="ExternalInput").ap()
    lutb_h = nc.dram_tensor("lutb_h", [128, 16], F32, kind="ExternalInput").ap()
    mask_h = nc.dram_tensor("mask_h", [128, 8], F32, kind="ExternalInput").ap()
    ones_h = nc.dram_tensor("ones_h", [128, T], BF16, kind="ExternalInput").ap()
    y_h = nc.dram_tensor("y_h", [16, 128, T], F32, kind="ExternalOutput").ap()
    if DEBUG:
        dbg_w = nc.dram_tensor("dbg_w", [17, 128, T], BF16, kind="ExternalOutput").ap()
        dbg_xn = nc.dram_tensor("dbg_xn", [16, 128, T], BF16, kind="ExternalOutput").ap()
        dbg_s = nc.dram_tensor("dbg_s", [128, 192], F32, kind="ExternalOutput").ap()
        dbg_s2 = nc.dram_tensor("dbg_s2", [128, 144], F32, kind="ExternalOutput").ap()

    cin, cout = [], []
    for j in range(1, NCHUNK + 1):
        a, b = CHUNKS[j - 1]
        cin.append(nc.dram_tensor(f"cc_in{j}", [128, b - a], BF16, kind="Internal").ap())
        cout.append(nc.dram_tensor(f"cc_out{j}", [NC, 128, b - a], BF16,
                                   kind="Internal", addr_space="Local").ap())
    cs_in = nc.dram_tensor("cs_in", [128, 16], F32, kind="Internal").ap()
    cs_out = nc.dram_tensor("cs_out", [NC, 128, 16], F32, kind="Internal",
                            addr_space="Local").ap()
    fl_in = nc.dram_tensor("fl_in", [1, 16], F32, kind="Internal").ap()
    fl_out = [nc.dram_tensor(f"fl_out{i}", [NC, 1, 16], F32, kind="Internal",
                             addr_space="Local").ap() for i in range(2)]
    mr_d = nc.dram_tensor("mr_d", [1, 2048], F32, kind="Internal").ap()
    rs_d = nc.dram_tensor("rs_d", [1, 2048], F32, kind="Internal").ap()

    es = ExitStack()
    def sbt(name, shape, dtype):
        return es.enter_context(nc.sbuf_tensor(name, shape, dtype))

    xn = sbt("xn", [128, NH * T], BF16)
    work = sbt("work", [128, 16 * T], BF16)
    big = sbt("big", [128, 28672], BF16)
    wkvb = sbt("wkvb", [128, 4 * D], BF16)
    sgst = sbt("sgst", [128, 2048], BF16)
    stats = sbt("stats", [128, 8 * 16], F32)
    rstd16 = sbt("rstd16", [128, 16], F32)
    s16 = sbt("s16", [128, 16], F32)
    sg = sbt("sg", [128, NC * 16], F32)
    c16 = sbt("c16", [128, 3 * 16], F32)
    maskb = sbt("maskb", [128, 8], F32)
    lutb = sbt("lutb", [128, 16], F32)
    dbgsg = sbt("dbgsg", [128, NC * 16 + 8], F32) if DEBUG else None

    def bigv(c0, cols, dtype=BF16):
        apb = big[:, c0:c0 + cols]
        return apb if dtype == BF16 else apb.bitcast(dtype)

    xtmv = [bigv(0, 4096, F32), bigv(4096, 4096, F32)]
    xftv = [bigv(8192, 4096, F32), bigv(12288, 4096, F32)]
    t1v = bigv(16384, 4096, F32)
    sqscr = t1v
    mbv = bigv(20480, 4096, F32)
    rbv = bigv(24576, 4096, F32)
    selidx = [bigv(0, 5504), bigv(5504, 5504)]
    selw = bigv(11008, 5504)
    seltmp = bigv(16512, 1024)
    selacc = [bigv(17536, 1024), bigv(18560, 1024)]
    kdsb = [bigv(19584, 2048), bigv(21632, 2048)]
    spare = bigv(23680, 2048)
    ystg = [bigv(0, 2048, F32), bigv(2048, 2048, F32)]
    xrl = [bigv(4096, 2048, F32), bigv(6144, 2048, F32)]
    gbuf = [bigv(8192, 2048), bigv(10240, 2048)]
    ubuf = [bigv(12288, 2048), bigv(14336, 2048)]
    dbuf = [bigv(16384, 1920), bigv(18304, 1920)]
    ones1 = sgst[:, 0:2048]

    banks = [es.enter_context(nc.psum_tensor(f"ps{i}", [128, 512], F32))
             for i in range(8)]

    def sem(name):
        return es.enter_context(nc.semaphore(name))
    s_xtm = [sem("s_xtm0"), sem("s_xtm1")]
    s_xft = [sem("s_xft0"), sem("s_xft1")]
    s_bc = sem("s_bc")
    s_k = [sem("s_k0"), sem("s_k1")]
    s_v = [sem("s_v0"), sem("s_v1")]
    s_xr = [sem("s_xr0"), sem("s_xr1")]
    s_g = [sem("s_g0"), sem("s_g1")]
    s_u = [sem("s_u0"), sem("s_u1")]
    s_d = [sem("s_d0"), sem("s_d1")]
    s_idx = [sem("s_idx0"), sem("s_idx1")]
    s_ccs, s_mrs, s_const = sem("s_ccs"), sem("s_mrs"), sem("s_const")
    s_stcc, s_sts = sem("s_stcc"), sem("s_sts")
    s_sty = [sem("s_sty0"), sem("s_sty1")]
    s_cc, s_mm, s_ve, s_ac = sem("s_cc"), sem("s_mm"), sem("s_ve"), sem("s_ac")
    s_fin = sem("s_fin")

    def vslot(et):
        return 15 if et == 0 else et - 2

    def wslot(i, tc=None):
        if i < 0:
            base, c0 = spare, 0
        else:
            base, c0 = work, i * T
        if tc is None:
            return base[:, c0:c0 + T]
        return base[:, c0 + tc * 512: c0 + (tc + 1) * 512]

    with es:
        block = es.enter_context(nc.Block())

        @block.vector
        def _(v):
            cnt = [0]
            def vinc(inst, key):
                cnt[0] += 1
                assert VE[key] == cnt[0], (key, VE[key], cnt[0])
                inst.then_inc(s_ve, 1)
            v.memset(c16[:, 0:16], 0.0)
            vinc(v.drain(), ("init",))
            for t in range(16):
                v.wait_ge(s_xtm[t % 2], W_XTM(t))
                vinc(v.reduce_sum(stats[:, t:t + 1], xtmv[t % 2][:, :], axis=AX.X),
                     ("red", t))
            v.wait_ge(s_ac, AC[("sq", 15)])
            v.drain()
            v.tensor_scalar(out=stats[:, 32:48], in0=stats[:, 0:16],
                            scalar1=1.0 / D, scalar2=None, op0=OP.mult)
            v.tensor_scalar(out=stats[:, 48:64], in0=stats[:, 16:32],
                            scalar1=1.0 / D, scalar2=None, op0=OP.mult)
            v.drain()
            v.tensor_tensor(out=stats[:, 64:80], in0=stats[:, 32:48],
                            in1=stats[:, 32:48], op=OP.mult)
            v.drain()
            v.scalar_tensor_tensor(out=stats[:, 80:96], in0=stats[:, 64:80],
                                   scalar=-1.0, in1=stats[:, 48:64],
                                   op0=OP.mult, op1=OP.add)
            v.drain()
            vinc(v.tensor_scalar(out=stats[:, 96:112], in0=stats[:, 80:96],
                                 scalar1=EPS, scalar2=None, op0=OP.add), ("vare",))
            v.wait_ge(s_ac, AC[("sqrt",)])
            v.reciprocal(rstd16[:, :], stats[:, 112:128])
            vinc(v.drain(), ("stats",))
            for h in range(16):
                v.wait_ge(s_xft[h % 2], W_XFT(h))
                v.wait_ge(s_bc, W_BC)
                v.drain()
                v.tensor_tensor(out=t1v[:, :], in0=xftv[h % 2][:, :], in1=mbv[:, :],
                                op=OP.subtract)
                v.drain()
                vinc(v.tensor_tensor(out=xn[:, h * T:(h + 1) * T], in0=t1v[:, :],
                                     in1=rbv[:, :], op=OP.mult), ("xn", h))
            v.drain()
            def sel_piece(pi):
                j, col, w = PIECES[pi]
                a, b = CHUNKS[j - 1]
                slot = (j - 1) % 2
                ioff = col - a
                v.wait_ge(s_idx[slot], W_IDX(j))
                if col == a and j > 1:
                    v.wait_ge(s_stcc, W_STCC(j - 1))
                for l in range(16):
                    if l == 0:
                        v.tensor_scalar(out=selacc[0][:, 0:w],
                                        in0=selidx[slot][:, ioff:ioff + w],
                                        scalar1=0.0, scalar2=lutb[:, 0:1],
                                        op0=OP.is_equal, op1=OP.mult)
                        v.drain()
                    else:
                        v.tensor_scalar(out=seltmp[:, 0:w],
                                        in0=selidx[slot][:, ioff:ioff + w],
                                        scalar1=float(l), scalar2=lutb[:, l:l + 1],
                                        op0=OP.is_equal, op1=OP.mult)
                        v.drain()
                        v.tensor_tensor(out=selacc[l % 2][:, 0:w],
                                        in0=selacc[1 - (l % 2)][:, 0:w],
                                        in1=seltmp[:, 0:w], op=OP.add)
                        v.drain()
                v.tensor_copy(selw[:, ioff:ioff + w], selacc[1][:, 0:w])
                if CHUNK_CLOSE_PIECE[j] == pi:
                    vinc(v.drain(), ("selchunk", j))
                else:
                    v.drain()
            pi_done = [0]
            for et in range(16):
                for tc in range(4):
                    v.wait_ge(s_mm, MM_V(et, tc))
                    v.wait_ge(s_ac, AC[("kev", et, tc)])
                    vinc(v.scalar_tensor_tensor(
                        out=wslot(et, tc), in0=banks[4 + tc][:, :], scalar=DENOM,
                        in1=kdsb[et % 2][:, tc * 512:(tc + 1) * 512],
                        op0=OP.mult, op1=OP.mult), ("u", et, tc))
                v.drain()
                v.wait_ge(s_const, W_CONST)
                vinc(v.tensor_tensor_scan(
                    out=wslot(et - 1), data0=wslot(et), data1=ones1[:, :],
                    initial=0.0, op0=OP.add, op1=OP.mult), ("scan", et))
                v.drain()
                v.tensor_copy(s16[:, et:et + 1], wslot(et - 1)[:, T - 1:T])
                v.drain()
                while pi_done[0] < PIECE_CUM[et]:
                    sel_piece(pi_done[0])
                    pi_done[0] += 1
            vinc(v.drain(), ("s16",))
            while pi_done[0] < NPIECE:
                sel_piece(pi_done[0])
                pi_done[0] += 1
            v.wait_ge(s_ccs, W_CCS)
            for r in range(NC):
                src = c16[:, 0:16] if r == 0 else (
                    c16[:, 16:32] if (r - 1) % 2 == 0 else c16[:, 32:48])
                dst = c16[:, 16:32] if r % 2 == 0 else c16[:, 32:48]
                v.scalar_tensor_tensor(out=dst, in0=sg[:, r * 16:(r + 1) * 16],
                                       scalar=maskb[:, r:r + 1], in1=src,
                                       op0=OP.mult, op1=OP.add)
                v.drain()
            CARRY = c16[:, 32:48]
            if DEBUG:
                v.tensor_copy(dbgsg[:, 0:NC * 16], sg[:, :])
                v.tensor_copy(dbgsg[:, NC * 16:NC * 16 + 8], maskb[:, :])
            vinc(v.drain(), ("carry",))
            for et in range(16):
                for tc in range(4):
                    v.wait_ge(s_mm, MM_Q(et, tc))
                    qb = banks[tc + 4 * (et % 2)]
                    vinc(v.scalar_tensor_tensor(
                        out=wslot(vslot(et), tc), in0=wslot(et - 1, tc),
                        scalar=CARRY[:, et:et + 1], in1=qb[:, :],
                        op0=OP.add, op1=OP.mult), ("vhat", et, tc))
                v.drain()
            for ft in range(16):
                for tc in range(4):
                    hf = tc // 2
                    v.wait_ge(s_mm, MM_WO(ft, tc))
                    v.wait_ge(s_xr[hf], W_XRL(ft))
                    if ft >= 1:
                        v.wait_ge(s_sty[hf], W_YA(ft - 1))
                    b = banks[tc + 4 * (ft % 2)]
                    vinc(v.tensor_tensor(
                        out=ystg[hf][:, (tc % 2) * 512:(tc % 2 + 1) * 512],
                        in0=b[:, :], in1=xrl[hf][:, (tc % 2) * 512:(tc % 2 + 1) * 512],
                        op=OP.add), ("woev", ft, tc))
                v.drain()
            if phases < 2:
                return
            for s, sw in enumerate(SWEEPS):
                for i, ot in enumerate(sw):
                    for st in (1, 2):
                        base = 0 if st == 1 else 4
                        v.wait_ge(s_ac, AC[("silu", ot, st)])
                        v.wait_ge(s_mm, MM_FFN[("u%d" % st, ot)])
                        for k in range(2):
                            tc = (st - 1) * 2 + k
                            inst = v.tensor_tensor(
                                out=wslot(i, tc),
                                in0=sgst[:, (st - 1) * 1024 + k * 512:
                                         (st - 1) * 1024 + (k + 1) * 512],
                                in1=banks[base + 2 + k][:, :], op=OP.mult)
                            if k == 1:
                                vinc(inst, ("hm", ot, st))
                        v.drain()
                for ft in range(16):
                    for tc in range(4):
                        hf = tc // 2
                        v.wait_ge(s_mm, MM_FFN[("d", s, ft, tc)])
                        v.wait_ge(s_xr[hf], W_YRL(s, ft))
                        if ft >= 1:
                            v.wait_ge(s_sty[hf], W_YD(s, ft - 1))
                        elif s > 0:
                            v.wait_ge(s_sty[hf], W_YD(s - 1, 15))
                        else:
                            v.wait_ge(s_sty[hf], W_YA(15))
                        b = banks[tc + 4 * (ft % 2)]
                        vinc(v.tensor_tensor(
                            out=ystg[hf][:, (tc % 2) * 512:(tc % 2 + 1) * 512],
                            in0=b[:, :],
                            in1=xrl[hf][:, (tc % 2) * 512:(tc % 2 + 1) * 512],
                            op=OP.add), ("dev", s, ft, tc))
                    v.drain()

        @block.scalar
        def _(sc):
            cnt = [0]
            def ainc(inst, key):
                cnt[0] += 1
                assert AC[key] == cnt[0], (key, AC[key], cnt[0])
                inst.then_inc(s_ac, 1)
            for t in range(16):
                sc.wait_ge(s_xtm[t % 2], W_XTM(t))
                ainc(sc.activation(sqscr[:, :], xtmv[t % 2][:, :], AF.Square,
                                   accum_out=stats[:, 16 + t:17 + t]), ("sq", t))
                sc.drain()
            sc.wait_ge(s_ve, VE[("vare",)])
            ainc(sc.activation(stats[:, 112:128], stats[:, 96:112], AF.Sqrt),
                 ("sqrt",))
            sc.wait_ge(s_ve, VE[("stats",)])
            sc.dma_start(out=mr_d.rearrange("a (p c) -> (a p) c", p=128),
                         in_=stats[:, 32:48]).then_inc(s_mrs, 16)
            sc.dma_start(out=rs_d.rearrange("a (p c) -> (a p) c", p=128),
                         in_=rstd16[:, :]).then_inc(s_mrs, 16)
            sc.dma_start(out=fl_in[:, :], in_=rstd16[0:1, 0:16]).then_inc(s_mrs, 16)
            pi_done = 0
            for et in range(16):
                for tc in range(4):
                    sc.wait_ge(s_mm, MM_K(et, tc))
                    if et >= 2:
                        sc.wait_ge(s_ve, VE[("u", et - 2, 3)])
                    ainc(sc.copy(kdsb[et % 2][:, tc * 512:(tc + 1) * 512],
                                 banks[tc][:, :]), ("kev", et, tc))
                while pi_done < PIECE_CUM[et]:
                    j = PIECES[pi_done][0]
                    if CHUNK_CLOSE_PIECE[j] == pi_done:
                        a, b = CHUNKS[j - 1]
                        sc.wait_ge(s_ve, VE[("selchunk", j)])
                        sc.dma_start(out=cin[j - 1][:, :],
                                     in_=selw[:, 0:b - a]).then_inc(s_stcc, 16)
                    pi_done += 1
            sc.wait_ge(s_ve, VE[("s16",)])
            sc.dma_start(out=cs_in[:, :], in_=s16[:, :]).then_inc(s_sts, 16)
            while pi_done < NPIECE:
                j = PIECES[pi_done][0]
                if CHUNK_CLOSE_PIECE[j] == pi_done:
                    a, b = CHUNKS[j - 1]
                    sc.wait_ge(s_ve, VE[("selchunk", j)])
                    sc.dma_start(out=cin[j - 1][:, :],
                                 in_=selw[:, 0:b - a]).then_inc(s_stcc, 16)
                pi_done += 1
            for ft in range(16):
                for hf in range(2):
                    sc.wait_ge(s_ve, VE[("woev", ft, 2 * hf + 1)])
                    sc.dma_start(out=y_h[ft, :, hf * 1024:(hf + 1) * 1024],
                                 in_=ystg[hf][:, :]).then_inc(s_sty[hf], 16)
            if phases < 2:
                sc.wait_ge(s_sty[0], W_YA(15))
                sc.wait_ge(s_sty[1], W_YA(15))
                if DEBUG:
                    sc.dma_start(out=dbg_w[0, :, :], in_=spare[:, :]).then_inc(s_fin, 16)
                    for i in range(16):
                        sc.dma_start(out=dbg_w[1 + i, :, :],
                                     in_=work[:, i * T:(i + 1) * T]).then_inc(s_fin, 16)
                    for h in range(16):
                        sc.dma_start(out=dbg_xn[h, :, :],
                                     in_=xn[:, h * T:(h + 1) * T]).then_inc(s_fin, 16)
                    sc.dma_start(out=dbg_s[:, 0:16], in_=s16[:, :]).then_inc(s_fin, 16)
                    sc.dma_start(out=dbg_s[:, 16:144], in_=sg[:, :]).then_inc(s_fin, 16)
                    sc.dma_start(out=dbg_s[:, 144:192], in_=c16[:, :]).then_inc(s_fin, 16)
                    sc.dma_start(out=dbg_s2[:, 0:128], in_=stats[:, :]).then_inc(s_fin, 16)
                    sc.dma_start(out=dbg_s2[:, 128:144], in_=rstd16[:, :]).then_inc(s_fin, 16)
                    sc.wait_ge(s_fin, 16 * 38)
                return
            for s, sw in enumerate(SWEEPS):
                for i, ot in enumerate(sw):
                    for st in (1, 2):
                        base = 0 if st == 1 else 4
                        sc.wait_ge(s_mm, MM_FFN[("g%d" % st, ot)])
                        if ot > 0:
                            sc.wait_ge(s_ve, VE[("hm", ot - 1, st)])
                        for k in range(2):
                            inst = sc.activation(
                                sgst[:, (st - 1) * 1024 + k * 512:
                                     (st - 1) * 1024 + (k + 1) * 512],
                                banks[base + k][:, :], AF.Silu)
                            if k == 1:
                                ainc(inst, ("silu", ot, st))
                        sc.drain()
                for ft in range(16):
                    for hf in range(2):
                        sc.wait_ge(s_ve, VE[("dev", s, ft, 2 * hf + 1)])
                        sc.dma_start(out=y_h[ft, :, hf * 1024:(hf + 1) * 1024],
                                     in_=ystg[hf][:, :]).then_inc(s_sty[hf], 16)
            sc.wait_ge(s_sty[0], STY_FINAL)
            sc.wait_ge(s_sty[1], STY_FINAL)

        @block.gpsimd
        def _(g):
            for t in range(16):
                if t >= 2:
                    g.wait_ge(s_ve, VE[("red", t - 2)])
                    g.wait_ge(s_ac, AC[("sq", t - 2)])
                g.dma_start(out=xtmv[t % 2][:, :],
                            in_=xtm_h[t, :, :]).then_inc(s_xtm[t % 2], 16)
            g.wait_ge(s_mrs, W_MRS)
            g.dma_start(out=mbv[:, :],
                        in_=mr_d[0:1, :].partition_broadcast(128)).then_inc(s_bc, 16)
            g.dma_start(out=rbv[:, :],
                        in_=rs_d[0:1, :].partition_broadcast(128)).then_inc(s_bc, 16)
            for h in range(16):
                if h >= 2:
                    g.wait_ge(s_ve, VE[("xn", h - 2)])
                g.dma_start(out=xftv[h % 2][:, :],
                            in_=xft_h[h, :, :]).then_inc(s_xft[h % 2], 16)
            cc_at = {6: 1, 8: 2}
            for et in range(16):
                if et >= 2:
                    g.wait_ge(s_mm, MM_K(et - 2, 3))
                g.dma_start(out=wkvb[:, (et % 2) * D:(et % 2 + 1) * D],
                            in_=wk_h[et, :, :]).then_inc(s_k[et % 2], 16)
                if et >= 2:
                    g.wait_ge(s_mm, MM_V(et - 2, 3))
                g.dma_start(out=wkvb[:, (2 + et % 2) * D:(3 + et % 2) * D],
                            in_=wv_h[et, :, :]).then_inc(s_v[et % 2], 16)
                if et in cc_at:
                    j = cc_at[et]
                    g.wait_ge(s_stcc, W_STCC(j))
                    g.collective_compute(
                        "AllGather", OP.bypass,
                        replica_groups=[list(range(NC))],
                        ins=[cin[j - 1][:, :]],
                        outs=[cout[j - 1][:, :, :]]).then_inc(s_cc, 1)
            for et in range(16):
                if et < 2:
                    g.wait_ge(s_mm, MM_K(14 + et, 3))
                else:
                    g.wait_ge(s_mm, MM_Q(et - 2, 3))
                g.dma_start(out=wkvb[:, (et % 2) * D:(et % 2 + 1) * D],
                            in_=wq_h[et, :, :]).then_inc(s_k[et % 2], 16)
                if et == 1:
                    g.wait_ge(s_sts, W_STS)
                    g.collective_compute(
                        "AllGather", OP.bypass,
                        replica_groups=[list(range(NC))],
                        ins=[cs_in[:, :]],
                        outs=[cs_out[:, :, :]]).then_inc(s_cc, 1)
                    g.wait_ge(s_mrs, W_MRS)
                    g.collective_compute(
                        "AllGather", OP.bypass,
                        replica_groups=[list(range(NC))],
                        ins=[fl_in[:, :]],
                        outs=[fl_out[0][:, :, :]]).then_inc(s_cc, 1)
            for ft in range(16):
                if ft in (0, 2, 4, 6, 8, 10):
                    j = 3 + ft // 2
                    g.wait_ge(s_stcc, W_STCC(j))
                    g.collective_compute(
                        "AllGather", OP.bypass,
                        replica_groups=[list(range(NC))],
                        ins=[cin[j - 1][:, :]],
                        outs=[cout[j - 1][:, :, :]]).then_inc(s_cc, 1)
                    if j == 8:
                        g.collective_compute(
                            "AllGather", OP.bypass,
                            replica_groups=[list(range(NC))],
                            ins=[fl_in[:, :]],
                            outs=[fl_out[1][:, :, :]]).then_inc(s_cc, 1)
                for hf in range(2):
                    if ft == 0:
                        g.wait_ge(s_stcc, W_STCC(8))
                    else:
                        g.wait_ge(s_ve, VE[("woev", ft - 1, 2 * hf + 1)])
                    g.dma_start(out=xrl[hf][:, :],
                                in_=xft_h[ft, :, hf * 1024:(hf + 1) * 1024]
                                ).then_inc(s_xr[hf], 16)
                if ft < 2:
                    g.wait_ge(s_mm, MM_V(14 + ft, 3))
                else:
                    g.wait_ge(s_mm, MM_WO(ft - 2, 3))
                g.dma_start(out=wkvb[:, (2 + ft % 2) * D:(3 + ft % 2) * D],
                            in_=wo_h[ft, :, :]).then_inc(s_v[ft % 2], 16)
            if phases < 2:
                return
            for s in range(3):
                for ft in range(16):
                    for hf in range(2):
                        if s == 0:
                            g.wait_ge(s_sty[hf], W_YA(ft))
                        else:
                            g.wait_ge(s_sty[hf], W_YD(s - 1, ft))
                        if ft >= 1:
                            g.wait_ge(s_ve, VE[("dev", s, ft - 1, 2 * hf + 1)])
                        elif s > 0:
                            g.wait_ge(s_ve, VE[("dev", s - 1, 15, 2 * hf + 1)])
                        else:
                            g.wait_ge(s_ve, VE[("woev", 15, 2 * hf + 1)])
                        g.dma_start(out=xrl[hf][:, :],
                                    in_=y_h[ft, :, hf * 1024:(hf + 1) * 1024]
                                    ).then_inc(s_xr[hf], 16)

        @block.sync
        def _(sy):
            sy.dma_start(out=lutb[:, :], in_=lutb_h[:, :]).then_inc(s_const, 16)
            sy.dma_start(out=maskb[:, :], in_=mask_h[:, :]).then_inc(s_const, 16)
            sy.dma_start(out=ones1[:, :], in_=ones_h[:, :]).then_inc(s_const, 16)
            for j in range(1, NCHUNK + 1):
                a, b = CHUNKS[j - 1]
                if j > 2:
                    sy.wait_ge(s_ve, VE[("selchunk", j - 2)])
                else:
                    sy.wait_ge(s_ve, VE[("xn", 15)])
                sy.dma_start(out=selidx[(j - 1) % 2][:, 0:b - a],
                             in_=idx_h[:, a:b]).then_inc(s_idx[(j - 1) % 2], 16)
            sy.wait_ge(s_cc, CC_DONE["carry"])
            for r in range(NC):
                sy.dma_start(out=sg[:, r * 16:(r + 1) * 16],
                             in_=cs_out[r, :, :]).then_inc(s_ccs, 16)
            if phases < 2:
                return
            for s, sw in enumerate(SWEEPS):
                for ot in sw:
                    rg, colg = _gate_loc(ot)
                    ru, colu = _up_loc(ot)
                    jg, ju = _chunk_of_col(colg), _chunk_of_col(colu)
                    ag = CHUNKS[jg - 1][0]
                    au = CHUNKS[ju - 1][0]
                    sy.wait_ge(s_cc, CC_DONE[jg])
                    if ot < 2:
                        sy.wait_ge(s_stcc, W_STCC(8))
                    else:
                        sy.wait_ge(s_mm, MM_FFN[("g2", ot - 2)])
                    sy.dma_start(out=gbuf[ot % 2][:, :],
                                 in_=cout[jg - 1][rg, :, colg - ag:colg - ag + 2048]
                                 ).then_inc(s_g[ot % 2], 16)
                    sy.wait_ge(s_cc, CC_DONE[ju])
                    if ot < 2:
                        sy.wait_ge(s_stcc, W_STCC(8))
                    else:
                        sy.wait_ge(s_mm, MM_FFN[("u2", ot - 2)])
                    sy.dma_start(out=ubuf[ot % 2][:, :],
                                 in_=cout[ju - 1][ru, :, colu - au:colu - au + 2048]
                                 ).then_inc(s_u[ot % 2], 16)
                nsw = len(sw)
                it0 = sw[0]
                for ft in range(16):
                    rd, cold = _down_loc(ft)
                    jd = _chunk_of_col(cold)
                    ad = CHUNKS[jd - 1][0]
                    sy.wait_ge(s_cc, CC_DONE[jd])
                    if s == 0 and ft < 2:
                        sy.wait_ge(s_ve, VE[("u", 15, 3)])
                        sy.wait_ge(s_stcc, W_STCC(8))
                    if ft >= 2:
                        sy.wait_ge(s_mm, MM_FFN[("d", s, ft - 2, 3)])
                    elif s > 0:
                        sy.wait_ge(s_mm, MM_FFN[("d", s - 1, 14 + ft, 3)])
                    off = cold - ad + it0 * 128
                    sy.dma_start(out=dbuf[ft % 2][:, 0:nsw * 128],
                                 in_=cout[jd - 1][rd, :, off:off + nsw * 128]
                                 ).then_inc(s_d[ft % 2], 16)

        @block.tensor
        def _(te):
            cnt = [0]
            def minc(inst, val):
                cnt[0] += 1
                assert val == cnt[0], (val, cnt[0])
                inst.then_inc(s_mm, 1)
            te.wait_ge(s_ve, VE[("xn", 15)])
            for et in range(16):
                te.wait_ge(s_k[et % 2], W_K(et))
                if et >= 1:
                    for tc in range(4):
                        te.wait_ge(s_ac, AC[("kev", et - 1, tc)])
                last = {}
                for dd in range(16):
                    for tc in range(4):
                        last[tc] = te.matmul(
                            banks[tc][:, :],
                            wkvb[:, (et % 2) * D + dd * 128:(et % 2) * D + (dd + 1) * 128],
                            xn[:, dd * T + tc * 512:dd * T + (tc + 1) * 512],
                            start=(dd == 0), stop=(dd == 15))
                for tc in range(4):
                    minc(last[tc], MM_K(et, tc))
                te.wait_ge(s_v[et % 2], W_V(et))
                if et >= 1:
                    for tc in range(4):
                        te.wait_ge(s_ve, VE[("u", et - 1, tc)])
                for dd in range(16):
                    for tc in range(4):
                        last[tc] = te.matmul(
                            banks[4 + tc][:, :],
                            wkvb[:, (2 + et % 2) * D + dd * 128:(2 + et % 2) * D + (dd + 1) * 128],
                            xn[:, dd * T + tc * 512:dd * T + (tc + 1) * 512],
                            start=(dd == 0), stop=(dd == 15))
                for tc in range(4):
                    minc(last[tc], MM_V(et, tc))
            for et in range(16):
                te.wait_ge(s_k[et % 2], W_Q(et))
                if et == 0:
                    for tc in range(4):
                        te.wait_ge(s_ac, AC[("kev", 15, tc)])
                elif et == 1:
                    for tc in range(4):
                        te.wait_ge(s_ve, VE[("u", 15, tc)])
                else:
                    for tc in range(4):
                        te.wait_ge(s_ve, VE[("vhat", et - 2, tc)])
                last = {}
                for dd in range(16):
                    for tc in range(4):
                        last[tc] = te.matmul(
                            banks[tc + 4 * (et % 2)][:, :],
                            wkvb[:, (et % 2) * D + dd * 128:(et % 2) * D + (dd + 1) * 128],
                            xn[:, dd * T + tc * 512:dd * T + (tc + 1) * 512],
                            start=(dd == 0), stop=(dd == 15))
                for tc in range(4):
                    minc(last[tc], MM_Q(et, tc))
            te.wait_ge(s_ve, VE[("vhat", 15, 3)])
            for ft in range(16):
                te.wait_ge(s_v[ft % 2], W_WOC(ft))
                if ft < 2:
                    for tc in range(4):
                        te.wait_ge(s_ve, VE[("vhat", 14 + ft, tc)])
                else:
                    for tc in range(4):
                        te.wait_ge(s_ve, VE[("woev", ft - 2, tc)])
                last = {}
                for ee in range(16):
                    for tc in range(4):
                        last[tc] = te.matmul(
                            banks[tc + 4 * (ft % 2)][:, :],
                            wkvb[:, (2 + ft % 2) * D + ee * 128:(2 + ft % 2) * D + (ee + 1) * 128],
                            wslot(vslot(ee), tc),
                            start=(ee == 0), stop=(ee == 15))
                for tc in range(4):
                    minc(last[tc], MM_WO(ft, tc))
            if phases < 2:
                return
            for s, sw in enumerate(SWEEPS):
                for i, ot in enumerate(sw):
                    te.wait_ge(s_g[ot % 2], W_G(ot))
                    te.wait_ge(s_u[ot % 2], W_U(ot))
                    if s == 0 and i == 0:
                        te.wait_ge(s_ve, VE[("woev", 15, 3)])
                    elif i == 0:
                        te.wait_ge(s_ve, VE[("dev", s - 1, 15, 3)])
                    else:
                        te.wait_ge(s_ac, AC[("silu", ot - 1, 1)])
                        te.wait_ge(s_ve, VE[("hm", ot - 1, 1)])
                    lastg, lastu = {}, {}
                    for dd in range(16):
                        for k in range(2):
                            lastg[k] = te.matmul(
                                banks[k][:, :],
                                gbuf[ot % 2][:, dd * 128:(dd + 1) * 128],
                                xn[:, dd * T + k * 512:dd * T + (k + 1) * 512],
                                start=(dd == 0), stop=(dd == 15))
                        for k in range(2):
                            lastu[k] = te.matmul(
                                banks[2 + k][:, :],
                                ubuf[ot % 2][:, dd * 128:(dd + 1) * 128],
                                xn[:, dd * T + k * 512:dd * T + (k + 1) * 512],
                                start=(dd == 0), stop=(dd == 15))
                    minc(lastg[1], MM_FFN[("g1", ot)])
                    minc(lastu[1], MM_FFN[("u1", ot)])
                    if i > 0:
                        te.wait_ge(s_ac, AC[("silu", ot - 1, 2)])
                        te.wait_ge(s_ve, VE[("hm", ot - 1, 2)])
                    for dd in range(16):
                        for k in range(2):
                            lastg[k] = te.matmul(
                                banks[4 + k][:, :],
                                gbuf[ot % 2][:, dd * 128:(dd + 1) * 128],
                                xn[:, dd * T + (2 + k) * 512:dd * T + (3 + k) * 512],
                                start=(dd == 0), stop=(dd == 15))
                        for k in range(2):
                            lastu[k] = te.matmul(
                                banks[6 + k][:, :],
                                ubuf[ot % 2][:, dd * 128:(dd + 1) * 128],
                                xn[:, dd * T + (2 + k) * 512:dd * T + (3 + k) * 512],
                                start=(dd == 0), stop=(dd == 15))
                    minc(lastg[1], MM_FFN[("g2", ot)])
                    minc(lastu[1], MM_FFN[("u2", ot)])
                nsw = len(sw)
                te.wait_ge(s_ve, VE[("hm", sw[-1], 2)])
                for ft in range(16):
                    te.wait_ge(s_d[ft % 2], W_D(s, ft))
                    if ft == 0:
                        te.wait_ge(s_ac, AC[("silu", sw[-1], 1)])
                    elif ft == 1:
                        te.wait_ge(s_ac, AC[("silu", sw[-1], 2)])
                    else:
                        for tc in range(4):
                            te.wait_ge(s_ve, VE[("dev", s, ft - 2, tc)])
                    last = {}
                    for iti in range(nsw):
                        for tc in range(4):
                            last[tc] = te.matmul(
                                banks[tc + 4 * (ft % 2)][:, :],
                                dbuf[ft % 2][:, iti * 128:(iti + 1) * 128],
                                wslot(iti, tc),
                                start=(iti == 0), stop=(iti == nsw - 1))
                    for tc in range(4):
                        minc(last[tc], MM_FFN[("d", s, ft, tc)])

    return nc


def _blk(W):
    return np.ascontiguousarray(
        W.reshape(16, 128, 16, 128).transpose(0, 3, 2, 1).reshape(16, 128, D))


def kernel(L0_attn_x, L0_attn_Wq, L0_attn_Wk, L0_attn_Wv, L0_attn_Wo,
           L0_ffn_lut, L0_ffn_gate_idx, L0_ffn_up_idx, L0_ffn_down_idx):
    x = np.asarray(L0_attn_x, dtype=np.float32)
    Wq = np.asarray(L0_attn_Wq, dtype=np.float32)
    Wk = np.asarray(L0_attn_Wk, dtype=np.float32)
    Wv = np.asarray(L0_attn_Wv, dtype=np.float32)
    Wo = np.asarray(L0_attn_Wo, dtype=np.float32)
    lut = np.asarray(L0_ffn_lut, dtype=np.float32)
    gidx = np.asarray(L0_ffn_gate_idx)
    uidx = np.asarray(L0_ffn_up_idx)
    didx = np.asarray(L0_ffn_down_idx)

    wqb, wkb, wvb, wob = _blk(Wq), _blk(Wk), _blk(Wv), _blk(Wo)
    lutb = np.ascontiguousarray(np.broadcast_to(lut[None, :], (128, 16)))
    ones1 = np.ones((128, T), dtype=ml_dtypes.bfloat16)

    g_can = gidx.reshape(43, 128, 16, 128).transpose(0, 3, 2, 1).reshape(43, 128, 2048)
    u_can = uidx.reshape(43, 128, 16, 128).transpose(0, 3, 2, 1).reshape(43, 128, 2048)
    d_can = didx.reshape(16, 128, 43, 128).transpose(0, 3, 2, 1).reshape(16, 128, 5504)

    in_maps = []
    for c in range(NC):
        bb, jj = c // 4, c % 4
        xs = x[bb, jj * T:(jj + 1) * T, :]
        xtm = np.ascontiguousarray(xs.reshape(128, 16, D).transpose(1, 0, 2))
        xft = np.ascontiguousarray(xs.T.reshape(16, 128, T))
        reg = np.zeros((128, REG_COLS), np.float32)
        a0, b0 = G_OWN[c]
        for s, ot in enumerate(range(a0, b0)):
            reg[:, REG_G0 + s * 2048:REG_G0 + (s + 1) * 2048] = g_can[ot]
            reg[:, REG_U0 + s * 2048:REG_U0 + (s + 1) * 2048] = u_can[ot]
        for s, ft in enumerate((2 * c, 2 * c + 1)):
            reg[:, REG_D0 + s * 5504:REG_D0 + (s + 1) * 5504] = d_can[ft]
        regb = reg.astype(ml_dtypes.bfloat16)
        mask = np.zeros((128, 8), np.float32)
        for r in range(NC):
            if r // 4 == bb and (r % 4) < jj:
                mask[:, r] = 1.0
        in_maps.append({
            "xtm_h": xtm, "xft_h": xft,
            "wq_h": wqb, "wk_h": wkb, "wv_h": wvb, "wo_h": wob,
            "idx_h": regb, "lutb_h": lutb, "mask_h": mask, "ones_h": ones1,
        })

    nc = _build_nc()
    res = run_bass_kernel_spmd(nc, in_maps, core_ids=list(range(NC)))
    out = np.empty((2, 8192, D), np.float32)
    for c in range(NC):
        bb, jj = c // 4, c % 4
        yf = np.asarray(res.results[c]["y_h"]).reshape(D, T)
        out[bb, jj * T:(jj + 1) * T, :] = yf.T
    return out


# version 04
# speedup vs baseline: 1.0982x; 1.0798x over previous
import numpy as np
import ml_dtypes
from contextlib import ExitStack

import concourse.bass as bass
import concourse.mybir as mybir
from concourse.bass_utils import run_bass_kernel_spmd

F32 = mybir.dt.float32
BF16 = mybir.dt.bfloat16
OP = mybir.AluOpType
AF = mybir.ActivationFunctionType
AX = mybir.AxisListType

NC = 8
T = 2048
D = 2048
NH = 16
DENOM = float(np.sqrt(2048.0) / 8192.0)
EPS = 1e-5
PHASES = 2
DEBUG = False

G_OWN = [(0, 6), (6, 12), (12, 17), (17, 22), (22, 28), (28, 33), (33, 38), (38, 43)]
GPAD = 6
REG_G0 = 0
REG_U0 = GPAD * 2048
REG_D0 = 2 * GPAD * 2048
REG_COLS = REG_D0 + 2 * 5504
CHUNKS = [(0, 4096), (4096, 8192), (8192, 12288),
          (12288, 16384), (16384, 20480), (20480, 24576),
          (24576, 30080), (30080, 35584)]
NCHUNK = len(CHUNKS)
CC_VAL = {1: 1, 2: 2, "carry": 3, "fl1": 4, 3: 5, 4: 6, "fl3": 7, 5: 8, 6: 9,
          7: 10, 8: 11, "fl2": 12}
CC_DONE = {1: 2, 2: 3, "carry": 4, 3: 6, 4: 7, 5: 9, 6: 10, 7: 11, 8: 12}

SWEEPS = [list(range(0, 15)), list(range(15, 29)), list(range(29, 43))]

def _gate_loc(ot):
    for c, (a, b) in enumerate(G_OWN):
        if a <= ot < b:
            return c, (ot - a) * 4096
    raise AssertionError(ot)

def _up_loc(ot):
    c, col = _gate_loc(ot)
    return c, col + 2048

def _down_loc(ft):
    return ft // 2, REG_D0 + (ft % 2) * 5504

def _chunk_of_col(col):
    for j, (a, b) in enumerate(CHUNKS):
        if a <= col < b:
            return j + 1
    raise AssertionError(col)

def _mk_pieces():
    out = []
    for j, (a, b) in enumerate(CHUNKS):
        col = a
        while col < b:
            w = min(1024, b - col)
            out.append((j + 1, col, w))
            col += w
    return out

PIECES = _mk_pieces()
NPIECE = len(PIECES)
PIECE_CUM = [min(24, int(np.ceil((et + 1) * 1.5))) for et in range(16)]
CHUNK_CLOSE_PIECE = {}
for _pi, (_j, _c, _w) in enumerate(PIECES):
    if _pi + 1 == NPIECE or PIECES[_pi + 1][0] != _j:
        CHUNK_CLOSE_PIECE[_j] = _pi

def W_XTM(t):      return 16 * (t // 2 + 1)
def W_XFT(h):      return 16 * (h // 2 + 1)
W_BC = 32
def W_K(et):       return 16 * (et // 2 + 1)
def W_Q(et):       return 16 * (8 + et // 2 + 1)
def W_V(et):       return 16 * (et // 2 + 1)
def W_WOC(ft):     return 16 * (8 + ft // 2 + 1)
def W_XRL(ft):     return 16 * (ft + 1)
def W_YRL(s, ft):  return 16 * (16 + 16 * s + ft + 1)
def W_G(ot):       return 16 * (ot // 2 + 1)
def W_U(ot):       return 16 * (ot // 2 + 1)
def W_D(s, ft):    return 16 * ((16 * s + ft) // 2 + 1)
def W_IDX(j):      return 16 * ((j - 1) // 2 + 1)
W_CCS = 16 * NC
W_MRS = 16 * 3
W_CONST = 16 * 2
def W_STCC(j):     return 16 * j
W_STS = 16
def W_YA(ft):      return 16 * (ft + 1)
def W_YD(s, ft):   return 16 * (16 + 16 * s + ft + 1)
STY_FINAL = 16 * (16 + 48)

def MM_K(et, tc):
    return 8 * et + tc + 1

def MM_V(et, tc):
    return 8 * et + 4 + tc + 1

MM_A_END = 128

def MM_Q(et, tc):
    return MM_A_END + 4 * et + tc + 1

MM_Q_END = MM_A_END + 64

def MM_WO(ft, tc):
    return MM_Q_END + 4 * ft + tc + 1

MM_WO_END = MM_Q_END + 64

def _mm_ffn_plan():
    val = MM_WO_END
    plan = {}
    for s, sw in enumerate(SWEEPS):
        for ot in sw:
            for tag in ("g1", "u1", "g2", "u2"):
                val += 1
                plan[(tag, ot)] = val
        for ft in range(16):
            for tc in range(4):
                val += 1
                plan[("d", s, ft, tc)] = val
    return plan

MM_FFN = _mm_ffn_plan()

def _ac_plan():
    val = 0
    plan = {}
    def put(k):
        nonlocal val
        val += 1
        plan[k] = val
    for t in range(16):
        put(("sq", t))
    put(("sqrt",))
    for et in range(16):
        for tc in range(4):
            put(("kev", et, tc))
    for s, sw in enumerate(SWEEPS):
        for ot in sw:
            put(("silu", ot, 1))
            put(("silu", ot, 2))
    return plan

AC = _ac_plan()

def _ve_plan():
    val = 0
    plan = {}
    def put(k):
        nonlocal val
        val += 1
        plan[k] = val
    put(("init",))
    for t in range(16):
        put(("red", t))
    put(("vare",))
    put(("stats",))
    for h in range(16):
        put(("xn", h))
    pi = 0
    for et in range(16):
        for tc in range(4):
            put(("u", et, tc))
        put(("scan", et))
        while pi < PIECE_CUM[et]:
            j = PIECES[pi][0]
            if CHUNK_CLOSE_PIECE[j] == pi:
                put(("selchunk", j))
            pi += 1
    put(("s16",))
    put(("carry",))
    for et in range(16):
        for tc in range(4):
            put(("vhat", et, tc))
    for ft in range(16):
        for tc in range(4):
            put(("woev", ft, tc))
        if pi < NPIECE:
            j = PIECES[pi][0]
            if CHUNK_CLOSE_PIECE[j] == pi:
                put(("selchunk", j))
            pi += 1
    for s, sw in enumerate(SWEEPS):
        for ot in sw:
            put(("hm", ot, 1))
            put(("hm", ot, 2))
        for ft in range(16):
            for tc in range(4):
                put(("dev", s, ft, tc))
    return plan

VE = _ve_plan()


def _build_nc(phases=PHASES):
    nc = bass.Bass("TRN2", target_bir_lowering=False, debug=False)

    xtm_h = nc.dram_tensor("xtm_h", [16, 128, D], F32, kind="ExternalInput").ap()
    xft_h = nc.dram_tensor("xft_h", [16, 128, T], F32, kind="ExternalInput").ap()
    wq_h = nc.dram_tensor("wq_h", [16, 128, D], F32, kind="ExternalInput").ap()
    wk_h = nc.dram_tensor("wk_h", [16, 128, D], F32, kind="ExternalInput").ap()
    wv_h = nc.dram_tensor("wv_h", [16, 128, D], F32, kind="ExternalInput").ap()
    wo_h = nc.dram_tensor("wo_h", [16, 128, D], F32, kind="ExternalInput").ap()
    idx_h = nc.dram_tensor("idx_h", [128, REG_COLS], BF16, kind="ExternalInput").ap()
    lutb_h = nc.dram_tensor("lutb_h", [128, 16], F32, kind="ExternalInput").ap()
    mask_h = nc.dram_tensor("mask_h", [128, 8], F32, kind="ExternalInput").ap()
    y_h = nc.dram_tensor("y_h", [16, 128, T], F32, kind="ExternalOutput").ap()
    if DEBUG:
        dbg_w = nc.dram_tensor("dbg_w", [17, 128, T], BF16, kind="ExternalOutput").ap()
        dbg_xn = nc.dram_tensor("dbg_xn", [16, 128, T], BF16, kind="ExternalOutput").ap()
        dbg_s = nc.dram_tensor("dbg_s", [128, 192], F32, kind="ExternalOutput").ap()
        dbg_s2 = nc.dram_tensor("dbg_s2", [128, 144], F32, kind="ExternalOutput").ap()

    cin, cout = [], []
    for j in range(1, NCHUNK + 1):
        a, b = CHUNKS[j - 1]
        cin.append(nc.dram_tensor(f"cc_in{j}", [128, b - a], BF16, kind="Internal").ap())
        cout.append(nc.dram_tensor(f"cc_out{j}", [NC, 128, b - a], BF16,
                                   kind="Internal", addr_space="Local").ap())
    cs_in = nc.dram_tensor("cs_in", [128, 16], F32, kind="Internal").ap()
    cs_out = nc.dram_tensor("cs_out", [NC, 128, 16], F32, kind="Internal",
                            addr_space="Local").ap()
    fl_in = nc.dram_tensor("fl_in", [1, 16], F32, kind="Internal").ap()
    fl_out = [nc.dram_tensor(f"fl_out{i}", [NC, 1, 16], F32, kind="Internal",
                             addr_space="Local").ap() for i in range(3)]
    mr_d = nc.dram_tensor("mr_d", [1, 2048], F32, kind="Internal").ap()
    rs_d = nc.dram_tensor("rs_d", [1, 2048], F32, kind="Internal").ap()

    es = ExitStack()
    def sbt(name, shape, dtype):
        return es.enter_context(nc.sbuf_tensor(name, shape, dtype))

    xn = sbt("xn", [128, NH * T], BF16)
    work = sbt("work", [128, 16 * T], BF16)
    big = sbt("big", [128, 31872], BF16)
    wkvb = sbt("wkvb", [128, 4 * D], BF16)
    stats = sbt("stats", [128, 8 * 16], F32)
    rstd16 = sbt("rstd16", [128, 16], F32)
    s16 = sbt("s16", [128, 16], F32)
    sg = sbt("sg", [128, NC * 16], F32)
    c16 = sbt("c16", [128, 3 * 16], F32)
    maskb = sbt("maskb", [128, 8], F32)
    lutb = sbt("lutb", [128, 16], F32)
    dbgsg = sbt("dbgsg", [128, NC * 16 + 8], F32) if DEBUG else None

    def bigv(c0, cols, dtype=BF16):
        apb = big[:, c0:c0 + cols]
        return apb if dtype == BF16 else apb.bitcast(dtype)

    xtmv = [bigv(0, 4096, F32), bigv(4096, 4096, F32)]
    xftv = [bigv(8192, 4096, F32), bigv(12288, 4096, F32)]
    t1v = bigv(16384, 4096, F32)
    sqscr = t1v
    mbv = bigv(20480, 4096, F32)
    rbv = bigv(24576, 4096, F32)
    selidx = [bigv(0, 5504), bigv(5504, 5504)]
    selw = bigv(11008, 5504)
    seltmp = bigv(16512, 1024)
    selacc = [bigv(17536, 1024), bigv(18560, 1024)]
    sgst = bigv(16512, 2048)
    kdsb = [bigv(19584, 2048), bigv(21632, 2048)]
    spare = bigv(23680, 2048)
    vtile0 = bigv(19584, 2048)
    vtile1 = bigv(21632, 2048)
    xrl = [bigv(23680, 2048, F32), bigv(25728, 2048, F32)]
    ystg = [bigv(27776, 2048, F32), bigv(29824, 2048, F32)]
    gbuf = [bigv(0, 2048), bigv(2048, 2048)]
    ubuf = [bigv(4096, 2048), bigv(6144, 2048)]
    dbuf = [bigv(8192, 1920), bigv(10112, 1920)]


    banks = [es.enter_context(nc.psum_tensor(f"ps{i}", [128, 512], F32))
             for i in range(8)]

    def sem(name):
        return es.enter_context(nc.semaphore(name))
    s_xtm = [sem("s_xtm0"), sem("s_xtm1")]
    s_xft = [sem("s_xft0"), sem("s_xft1")]
    s_bc = sem("s_bc")
    s_k = [sem("s_k0"), sem("s_k1")]
    s_v = [sem("s_v0"), sem("s_v1")]
    s_xr = [sem("s_xr0"), sem("s_xr1")]
    s_g = [sem("s_g0"), sem("s_g1")]
    s_u = [sem("s_u0"), sem("s_u1")]
    s_d = [sem("s_d0"), sem("s_d1")]
    s_idx = [sem("s_idx0"), sem("s_idx1")]
    s_ccs, s_mrs, s_const = sem("s_ccs"), sem("s_mrs"), sem("s_const")
    s_stcc, s_sts = sem("s_stcc"), sem("s_sts")
    s_sty = [sem("s_sty0"), sem("s_sty1")]
    s_cc, s_mm, s_ve, s_ac = sem("s_cc"), sem("s_mm"), sem("s_ve"), sem("s_ac")
    s_fin = sem("s_fin")

    def vslot(et):
        return ("v0" if et == 0 else ("v1" if et == 1 else et - 2))

    def wslot(i, tc=None):
        if i == "v0":
            base, c0 = vtile0, 0
        elif i == "v1":
            base, c0 = vtile1, 0
        elif i < 0:
            base, c0 = spare, 0
        else:
            base, c0 = work, i * T
        if tc is None:
            return base[:, c0:c0 + T]
        return base[:, c0 + tc * 512: c0 + (tc + 1) * 512]

    with es:
        block = es.enter_context(nc.Block())

        @block.vector
        def _(v):
            cnt = [0]
            def vinc(inst, key):
                cnt[0] += 1
                assert VE[key] == cnt[0], (key, VE[key], cnt[0])
                inst.then_inc(s_ve, 1)
            v.memset(c16[:, 0:16], 0.0)
            vinc(v.drain(), ("init",))
            for t in range(16):
                v.wait_ge(s_xtm[t % 2], W_XTM(t))
                vinc(v.reduce_sum(stats[:, t:t + 1], xtmv[t % 2][:, :], axis=AX.X),
                     ("red", t))
            v.wait_ge(s_ac, AC[("sq", 15)])
            v.drain()
            v.tensor_scalar(out=stats[:, 32:48], in0=stats[:, 0:16],
                            scalar1=1.0 / D, scalar2=None, op0=OP.mult)
            v.tensor_scalar(out=stats[:, 48:64], in0=stats[:, 16:32],
                            scalar1=1.0 / D, scalar2=None, op0=OP.mult)
            v.drain()
            v.tensor_tensor(out=stats[:, 64:80], in0=stats[:, 32:48],
                            in1=stats[:, 32:48], op=OP.mult)
            v.drain()
            v.scalar_tensor_tensor(out=stats[:, 80:96], in0=stats[:, 64:80],
                                   scalar=-1.0, in1=stats[:, 48:64],
                                   op0=OP.mult, op1=OP.add)
            v.drain()
            vinc(v.tensor_scalar(out=stats[:, 96:112], in0=stats[:, 80:96],
                                 scalar1=EPS, scalar2=None, op0=OP.add), ("vare",))
            v.wait_ge(s_ac, AC[("sqrt",)])
            v.reciprocal(rstd16[:, :], stats[:, 112:128])
            vinc(v.drain(), ("stats",))
            for h in range(16):
                v.wait_ge(s_xft[h % 2], W_XFT(h))
                v.wait_ge(s_bc, W_BC)
                v.drain()
                v.tensor_tensor(out=t1v[:, :], in0=xftv[h % 2][:, :], in1=mbv[:, :],
                                op=OP.subtract)
                v.drain()
                vinc(v.tensor_tensor(out=xn[:, h * T:(h + 1) * T], in0=t1v[:, :],
                                     in1=rbv[:, :], op=OP.mult), ("xn", h))
            v.drain()
            def sel_piece(pi):
                j, col, w = PIECES[pi]
                a, b = CHUNKS[j - 1]
                slot = (j - 1) % 2
                ioff = col - a
                v.wait_ge(s_idx[slot], W_IDX(j))
                if col == a and j > 1:
                    v.wait_ge(s_stcc, W_STCC(j - 1))
                for l in range(16):
                    if l == 0:
                        v.tensor_scalar(out=selacc[0][:, 0:w],
                                        in0=selidx[slot][:, ioff:ioff + w],
                                        scalar1=0.0, scalar2=lutb[:, 0:1],
                                        op0=OP.is_equal, op1=OP.mult)
                        v.drain()
                    else:
                        v.tensor_scalar(out=seltmp[:, 0:w],
                                        in0=selidx[slot][:, ioff:ioff + w],
                                        scalar1=float(l), scalar2=lutb[:, l:l + 1],
                                        op0=OP.is_equal, op1=OP.mult)
                        v.drain()
                        v.tensor_tensor(out=selacc[l % 2][:, 0:w],
                                        in0=selacc[1 - (l % 2)][:, 0:w],
                                        in1=seltmp[:, 0:w], op=OP.add)
                        v.drain()
                v.tensor_copy(selw[:, ioff:ioff + w], selacc[1][:, 0:w])
                if CHUNK_CLOSE_PIECE[j] == pi:
                    vinc(v.drain(), ("selchunk", j))
                else:
                    v.drain()
            pi_done = [0]
            for et in range(16):
                for tc in range(4):
                    v.wait_ge(s_mm, MM_V(et, tc))
                    v.wait_ge(s_ac, AC[("kev", et, tc)])
                    vinc(v.scalar_tensor_tensor(
                        out=wslot(et, tc), in0=banks[4 + tc][:, :], scalar=DENOM,
                        in1=kdsb[et % 2][:, tc * 512:(tc + 1) * 512],
                        op0=OP.mult, op1=OP.mult), ("u", et, tc))
                v.drain()
                v.wait_ge(s_const, W_CONST)
                vinc(v.tensor_tensor_scan(
                    out=wslot(et - 1), data0=wslot(et), data1=wslot(et),
                    initial=0.0, op0=OP.add, op1=OP.bypass), ("scan", et))
                v.drain()
                v.tensor_copy(s16[:, et:et + 1], wslot(et - 1)[:, T - 1:T])
                v.drain()
                while pi_done[0] < PIECE_CUM[et]:
                    sel_piece(pi_done[0])
                    pi_done[0] += 1
            vinc(v.drain(), ("s16",))
            v.wait_ge(s_ccs, W_CCS)
            for r in range(NC):
                src = c16[:, 0:16] if r == 0 else (
                    c16[:, 16:32] if (r - 1) % 2 == 0 else c16[:, 32:48])
                dst = c16[:, 16:32] if r % 2 == 0 else c16[:, 32:48]
                v.scalar_tensor_tensor(out=dst, in0=sg[:, r * 16:(r + 1) * 16],
                                       scalar=maskb[:, r:r + 1], in1=src,
                                       op0=OP.mult, op1=OP.add)
                v.drain()
            CARRY = c16[:, 32:48]
            if DEBUG:
                v.tensor_copy(dbgsg[:, 0:NC * 16], sg[:, :])
                v.tensor_copy(dbgsg[:, NC * 16:NC * 16 + 8], maskb[:, :])
            vinc(v.drain(), ("carry",))
            for et in range(16):
                for tc in range(4):
                    v.wait_ge(s_mm, MM_Q(et, tc))
                    qb = banks[tc + 4 * (et % 2)]
                    vinc(v.scalar_tensor_tensor(
                        out=wslot(vslot(et), tc), in0=wslot(et - 1, tc),
                        scalar=CARRY[:, et:et + 1], in1=qb[:, :],
                        op0=OP.add, op1=OP.mult), ("vhat", et, tc))
                v.drain()
            for ft in range(16):
                for tc in range(4):
                    hf = tc // 2
                    v.wait_ge(s_mm, MM_WO(ft, tc))
                    v.wait_ge(s_xr[hf], W_XRL(ft))
                    if ft >= 1:
                        v.wait_ge(s_sty[hf], W_YA(ft - 1))
                    b = banks[tc + 4 * (ft % 2)]
                    vinc(v.tensor_tensor(
                        out=ystg[hf][:, (tc % 2) * 512:(tc % 2 + 1) * 512],
                        in0=b[:, :], in1=xrl[hf][:, (tc % 2) * 512:(tc % 2 + 1) * 512],
                        op=OP.add), ("woev", ft, tc))
                v.drain()
                if pi_done[0] < NPIECE:
                    sel_piece(pi_done[0])
                    pi_done[0] += 1
            if phases < 2:
                return
            for s, sw in enumerate(SWEEPS):
                for i, ot in enumerate(sw):
                    for st in (1, 2):
                        base = 0 if st == 1 else 4
                        v.wait_ge(s_ac, AC[("silu", ot, st)])
                        v.wait_ge(s_mm, MM_FFN[("u%d" % st, ot)])
                        for k in range(2):
                            tc = (st - 1) * 2 + k
                            inst = v.tensor_tensor(
                                out=wslot(i, tc),
                                in0=sgst[:, (st - 1) * 1024 + k * 512:
                                         (st - 1) * 1024 + (k + 1) * 512],
                                in1=banks[base + 2 + k][:, :], op=OP.mult)
                            if k == 1:
                                vinc(inst, ("hm", ot, st))
                        v.drain()
                for ft in range(16):
                    for tc in range(4):
                        hf = tc // 2
                        v.wait_ge(s_mm, MM_FFN[("d", s, ft, tc)])
                        v.wait_ge(s_xr[hf], W_YRL(s, ft))
                        if ft >= 1:
                            v.wait_ge(s_sty[hf], W_YD(s, ft - 1))
                        elif s > 0:
                            v.wait_ge(s_sty[hf], W_YD(s - 1, 15))
                        else:
                            v.wait_ge(s_sty[hf], W_YA(15))
                        b = banks[tc + 4 * (ft % 2)]
                        vinc(v.tensor_tensor(
                            out=ystg[hf][:, (tc % 2) * 512:(tc % 2 + 1) * 512],
                            in0=b[:, :],
                            in1=xrl[hf][:, (tc % 2) * 512:(tc % 2 + 1) * 512],
                            op=OP.add), ("dev", s, ft, tc))
                    v.drain()

        @block.scalar
        def _(sc):
            cnt = [0]
            def ainc(inst, key):
                cnt[0] += 1
                assert AC[key] == cnt[0], (key, AC[key], cnt[0])
                inst.then_inc(s_ac, 1)
            for t in range(16):
                sc.wait_ge(s_xtm[t % 2], W_XTM(t))
                ainc(sc.activation(sqscr[:, :], xtmv[t % 2][:, :], AF.Square,
                                   accum_out=stats[:, 16 + t:17 + t]), ("sq", t))
                sc.drain()
            sc.wait_ge(s_ve, VE[("vare",)])
            ainc(sc.activation(stats[:, 112:128], stats[:, 96:112], AF.Sqrt),
                 ("sqrt",))
            sc.wait_ge(s_ve, VE[("stats",)])
            sc.dma_start(out=mr_d.rearrange("a (p c) -> (a p) c", p=128),
                         in_=stats[:, 32:48]).then_inc(s_mrs, 16)
            sc.dma_start(out=rs_d.rearrange("a (p c) -> (a p) c", p=128),
                         in_=rstd16[:, :]).then_inc(s_mrs, 16)
            sc.dma_start(out=fl_in[:, :], in_=rstd16[0:1, 0:16]).then_inc(s_mrs, 16)
            pi_done = 0
            for et in range(16):
                for tc in range(4):
                    sc.wait_ge(s_mm, MM_K(et, tc))
                    if et >= 2:
                        sc.wait_ge(s_ve, VE[("u", et - 2, 3)])
                    ainc(sc.copy(kdsb[et % 2][:, tc * 512:(tc + 1) * 512],
                                 banks[tc][:, :]), ("kev", et, tc))
                while pi_done < PIECE_CUM[et]:
                    j = PIECES[pi_done][0]
                    if CHUNK_CLOSE_PIECE[j] == pi_done:
                        a, b = CHUNKS[j - 1]
                        sc.wait_ge(s_ve, VE[("selchunk", j)])
                        sc.dma_start(out=cin[j - 1][:, :],
                                     in_=selw[:, 0:b - a]).then_inc(s_stcc, 16)
                    pi_done += 1
            sc.wait_ge(s_ve, VE[("s16",)])
            sc.dma_start(out=cs_in[:, :], in_=s16[:, :]).then_inc(s_sts, 16)
            for ft in range(16):
                for hf in range(2):
                    sc.wait_ge(s_ve, VE[("woev", ft, 2 * hf + 1)])
                    sc.dma_start(out=y_h[ft, :, hf * 1024:(hf + 1) * 1024],
                                 in_=ystg[hf][:, :]).then_inc(s_sty[hf], 16)
                if ft in (6, 13):
                    j = 7 if ft == 6 else 8
                    a, b = CHUNKS[j - 1]
                    sc.wait_ge(s_ve, VE[("selchunk", j)])
                    sc.dma_start(out=cin[j - 1][:, :],
                                 in_=selw[:, 0:b - a]).then_inc(s_stcc, 16)
            if phases < 2:
                sc.wait_ge(s_sty[0], W_YA(15))
                sc.wait_ge(s_sty[1], W_YA(15))
                if DEBUG:
                    sc.dma_start(out=dbg_w[0, :, :], in_=spare[:, :]).then_inc(s_fin, 16)
                    for i in range(16):
                        sc.dma_start(out=dbg_w[1 + i, :, :],
                                     in_=work[:, i * T:(i + 1) * T]).then_inc(s_fin, 16)
                    for h in range(16):
                        sc.dma_start(out=dbg_xn[h, :, :],
                                     in_=xn[:, h * T:(h + 1) * T]).then_inc(s_fin, 16)
                    sc.dma_start(out=dbg_s[:, 0:16], in_=s16[:, :]).then_inc(s_fin, 16)
                    sc.dma_start(out=dbg_s[:, 16:144], in_=sg[:, :]).then_inc(s_fin, 16)
                    sc.dma_start(out=dbg_s[:, 144:192], in_=c16[:, :]).then_inc(s_fin, 16)
                    sc.dma_start(out=dbg_s2[:, 0:128], in_=stats[:, :]).then_inc(s_fin, 16)
                    sc.dma_start(out=dbg_s2[:, 128:144], in_=rstd16[:, :]).then_inc(s_fin, 16)
                    sc.wait_ge(s_fin, 16 * 38)
                return
            for s, sw in enumerate(SWEEPS):
                for i, ot in enumerate(sw):
                    for st in (1, 2):
                        base = 0 if st == 1 else 4
                        sc.wait_ge(s_mm, MM_FFN[("g%d" % st, ot)])
                        if ot > 0:
                            sc.wait_ge(s_ve, VE[("hm", ot - 1, st)])
                        for k in range(2):
                            inst = sc.activation(
                                sgst[:, (st - 1) * 1024 + k * 512:
                                     (st - 1) * 1024 + (k + 1) * 512],
                                banks[base + k][:, :], AF.Silu)
                            if k == 1:
                                ainc(inst, ("silu", ot, st))
                        sc.drain()
                for ft in range(16):
                    for hf in range(2):
                        sc.wait_ge(s_ve, VE[("dev", s, ft, 2 * hf + 1)])
                        sc.dma_start(out=y_h[ft, :, hf * 1024:(hf + 1) * 1024],
                                     in_=ystg[hf][:, :]).then_inc(s_sty[hf], 16)
            sc.wait_ge(s_sty[0], STY_FINAL)
            sc.wait_ge(s_sty[1], STY_FINAL)

        @block.gpsimd
        def _(g):
            for t in range(16):
                if t >= 2:
                    g.wait_ge(s_ve, VE[("red", t - 2)])
                    g.wait_ge(s_ac, AC[("sq", t - 2)])
                g.dma_start(out=xtmv[t % 2][:, :],
                            in_=xtm_h[t, :, :]).then_inc(s_xtm[t % 2], 16)
            g.wait_ge(s_mrs, W_MRS)
            g.dma_start(out=mbv[:, :],
                        in_=mr_d[0:1, :].partition_broadcast(128)).then_inc(s_bc, 16)
            g.dma_start(out=rbv[:, :],
                        in_=rs_d[0:1, :].partition_broadcast(128)).then_inc(s_bc, 16)
            for h in range(16):
                if h >= 2:
                    g.wait_ge(s_ve, VE[("xn", h - 2)])
                g.dma_start(out=xftv[h % 2][:, :],
                            in_=xft_h[h, :, :]).then_inc(s_xft[h % 2], 16)
            cc_at = {6: 1, 8: 2}
            for et in range(16):
                if et >= 2:
                    g.wait_ge(s_mm, MM_K(et - 2, 3))
                g.dma_start(out=wkvb[:, (et % 2) * D:(et % 2 + 1) * D],
                            in_=wk_h[et, :, :]).then_inc(s_k[et % 2], 16)
                if et >= 2:
                    g.wait_ge(s_mm, MM_V(et - 2, 3))
                g.dma_start(out=wkvb[:, (2 + et % 2) * D:(3 + et % 2) * D],
                            in_=wv_h[et, :, :]).then_inc(s_v[et % 2], 16)
                if et in cc_at:
                    j = cc_at[et]
                    g.wait_ge(s_stcc, W_STCC(j))
                    g.collective_compute(
                        "AllGather", OP.bypass,
                        replica_groups=[list(range(NC))],
                        ins=[cin[j - 1][:, :]],
                        outs=[cout[j - 1][:, :, :]]).then_inc(s_cc, 1)
            for et in range(16):
                if et < 2:
                    g.wait_ge(s_mm, MM_K(14 + et, 3))
                else:
                    g.wait_ge(s_mm, MM_Q(et - 2, 3))
                g.dma_start(out=wkvb[:, (et % 2) * D:(et % 2 + 1) * D],
                            in_=wq_h[et, :, :]).then_inc(s_k[et % 2], 16)
                if et == 1:
                    g.wait_ge(s_sts, W_STS)
                    g.collective_compute(
                        "AllGather", OP.bypass,
                        replica_groups=[list(range(NC))],
                        ins=[cs_in[:, :]],
                        outs=[cs_out[:, :, :]]).then_inc(s_cc, 1)
                    g.wait_ge(s_mrs, W_MRS)
                    g.collective_compute(
                        "AllGather", OP.bypass,
                        replica_groups=[list(range(NC))],
                        ins=[fl_in[:, :]],
                        outs=[fl_out[0][:, :, :]]).then_inc(s_cc, 1)
                if et in (3, 5, 7, 9):
                    j = {3: 3, 5: 4, 7: 5, 9: 6}[et]
                    g.wait_ge(s_stcc, W_STCC(j))
                    g.collective_compute(
                        "AllGather", OP.bypass,
                        replica_groups=[list(range(NC))],
                        ins=[cin[j - 1][:, :]],
                        outs=[cout[j - 1][:, :, :]]).then_inc(s_cc, 1)
                    if j == 4:
                        g.collective_compute(
                            "AllGather", OP.bypass,
                            replica_groups=[list(range(NC))],
                            ins=[fl_in[:, :]],
                            outs=[fl_out[2][:, :, :]]).then_inc(s_cc, 1)
            for ft in range(16):
                if ft in (8, 14):
                    j = 7 if ft == 8 else 8
                    g.wait_ge(s_stcc, W_STCC(j))
                    g.collective_compute(
                        "AllGather", OP.bypass,
                        replica_groups=[list(range(NC))],
                        ins=[cin[j - 1][:, :]],
                        outs=[cout[j - 1][:, :, :]]).then_inc(s_cc, 1)
                    if j == 8:
                        g.collective_compute(
                            "AllGather", OP.bypass,
                            replica_groups=[list(range(NC))],
                            ins=[fl_in[:, :]],
                            outs=[fl_out[1][:, :, :]]).then_inc(s_cc, 1)
                for hf in range(2):
                    if ft == 0:
                        g.wait_ge(s_ve, VE[("vhat", 0, 3)])
                    elif False:
                        pass
                    else:
                        g.wait_ge(s_ve, VE[("woev", ft - 1, 2 * hf + 1)])
                    g.dma_start(out=xrl[hf][:, :],
                                in_=xft_h[ft, :, hf * 1024:(hf + 1) * 1024]
                                ).then_inc(s_xr[hf], 16)
                if ft < 2:
                    g.wait_ge(s_mm, MM_V(14 + ft, 3))
                else:
                    g.wait_ge(s_mm, MM_WO(ft - 2, 3))
                g.dma_start(out=wkvb[:, (2 + ft % 2) * D:(3 + ft % 2) * D],
                            in_=wo_h[ft, :, :]).then_inc(s_v[ft % 2], 16)
            if phases < 2:
                return
            for s in range(3):
                for ft in range(16):
                    for hf in range(2):
                        if s == 0:
                            g.wait_ge(s_sty[hf], W_YA(ft))
                        else:
                            g.wait_ge(s_sty[hf], W_YD(s - 1, ft))
                        if ft >= 1:
                            g.wait_ge(s_ve, VE[("dev", s, ft - 1, 2 * hf + 1)])
                        elif s > 0:
                            g.wait_ge(s_ve, VE[("dev", s - 1, 15, 2 * hf + 1)])
                        else:
                            g.wait_ge(s_ve, VE[("woev", 15, 2 * hf + 1)])
                        g.dma_start(out=xrl[hf][:, :],
                                    in_=y_h[ft, :, hf * 1024:(hf + 1) * 1024]
                                    ).then_inc(s_xr[hf], 16)

        @block.sync
        def _(sy):
            sy.dma_start(out=lutb[:, :], in_=lutb_h[:, :]).then_inc(s_const, 16)
            sy.dma_start(out=maskb[:, :], in_=mask_h[:, :]).then_inc(s_const, 16)
            for j in range(1, NCHUNK + 1):
                a, b = CHUNKS[j - 1]
                if j > 2:
                    sy.wait_ge(s_ve, VE[("selchunk", j - 2)])
                else:
                    sy.wait_ge(s_ve, VE[("xn", 15)])
                sy.dma_start(out=selidx[(j - 1) % 2][:, 0:b - a],
                             in_=idx_h[:, a:b]).then_inc(s_idx[(j - 1) % 2], 16)
            sy.wait_ge(s_cc, CC_DONE["carry"])
            for r in range(NC):
                sy.dma_start(out=sg[:, r * 16:(r + 1) * 16],
                             in_=cs_out[r, :, :]).then_inc(s_ccs, 16)
            if phases < 2:
                return
            for s, sw in enumerate(SWEEPS):
                for ot in sw:
                    rg, colg = _gate_loc(ot)
                    ru, colu = _up_loc(ot)
                    jg, ju = _chunk_of_col(colg), _chunk_of_col(colu)
                    ag = CHUNKS[jg - 1][0]
                    au = CHUNKS[ju - 1][0]
                    sy.wait_ge(s_cc, CC_DONE[jg])
                    if ot < 2:
                        sy.wait_ge(s_stcc, W_STCC(8))
                    else:
                        sy.wait_ge(s_mm, MM_FFN[("g2", ot - 2)])
                    sy.dma_start(out=gbuf[ot % 2][:, :],
                                 in_=cout[jg - 1][rg, :, colg - ag:colg - ag + 2048]
                                 ).then_inc(s_g[ot % 2], 16)
                    sy.wait_ge(s_cc, CC_DONE[ju])
                    if ot < 2:
                        sy.wait_ge(s_stcc, W_STCC(8))
                    else:
                        sy.wait_ge(s_mm, MM_FFN[("u2", ot - 2)])
                    sy.dma_start(out=ubuf[ot % 2][:, :],
                                 in_=cout[ju - 1][ru, :, colu - au:colu - au + 2048]
                                 ).then_inc(s_u[ot % 2], 16)
                nsw = len(sw)
                it0 = sw[0]
                for ft in range(16):
                    rd, cold = _down_loc(ft)
                    jd = _chunk_of_col(cold)
                    ad = CHUNKS[jd - 1][0]
                    sy.wait_ge(s_cc, CC_DONE[jd])
                    if s == 0 and ft < 2:
                        sy.wait_ge(s_ve, VE[("u", 15, 3)])
                        sy.wait_ge(s_stcc, W_STCC(8))
                    if ft >= 2:
                        sy.wait_ge(s_mm, MM_FFN[("d", s, ft - 2, 3)])
                    elif s > 0:
                        sy.wait_ge(s_mm, MM_FFN[("d", s - 1, 14 + ft, 3)])
                    off = cold - ad + it0 * 128
                    sy.dma_start(out=dbuf[ft % 2][:, 0:nsw * 128],
                                 in_=cout[jd - 1][rd, :, off:off + nsw * 128]
                                 ).then_inc(s_d[ft % 2], 16)

        @block.tensor
        def _(te):
            cnt = [0]
            def minc(inst, val):
                cnt[0] += 1
                assert val == cnt[0], (val, cnt[0])
                inst.then_inc(s_mm, 1)
            te.wait_ge(s_ve, VE[("xn", 15)])
            for et in range(16):
                te.wait_ge(s_k[et % 2], W_K(et))
                if et >= 1:
                    for tc in range(4):
                        te.wait_ge(s_ac, AC[("kev", et - 1, tc)])
                last = {}
                for dd in range(16):
                    for tc in range(4):
                        last[tc] = te.matmul(
                            banks[tc][:, :],
                            wkvb[:, (et % 2) * D + dd * 128:(et % 2) * D + (dd + 1) * 128],
                            xn[:, dd * T + tc * 512:dd * T + (tc + 1) * 512],
                            start=(dd == 0), stop=(dd == 15))
                for tc in range(4):
                    minc(last[tc], MM_K(et, tc))
                te.wait_ge(s_v[et % 2], W_V(et))
                if et >= 1:
                    for tc in range(4):
                        te.wait_ge(s_ve, VE[("u", et - 1, tc)])
                for dd in range(16):
                    for tc in range(4):
                        last[tc] = te.matmul(
                            banks[4 + tc][:, :],
                            wkvb[:, (2 + et % 2) * D + dd * 128:(2 + et % 2) * D + (dd + 1) * 128],
                            xn[:, dd * T + tc * 512:dd * T + (tc + 1) * 512],
                            start=(dd == 0), stop=(dd == 15))
                for tc in range(4):
                    minc(last[tc], MM_V(et, tc))
            for et in range(16):
                te.wait_ge(s_k[et % 2], W_Q(et))
                if et == 0:
                    for tc in range(4):
                        te.wait_ge(s_ac, AC[("kev", 15, tc)])
                elif et == 1:
                    for tc in range(4):
                        te.wait_ge(s_ve, VE[("u", 15, tc)])
                else:
                    for tc in range(4):
                        te.wait_ge(s_ve, VE[("vhat", et - 2, tc)])
                last = {}
                for dd in range(16):
                    for tc in range(4):
                        last[tc] = te.matmul(
                            banks[tc + 4 * (et % 2)][:, :],
                            wkvb[:, (et % 2) * D + dd * 128:(et % 2) * D + (dd + 1) * 128],
                            xn[:, dd * T + tc * 512:dd * T + (tc + 1) * 512],
                            start=(dd == 0), stop=(dd == 15))
                for tc in range(4):
                    minc(last[tc], MM_Q(et, tc))
            te.wait_ge(s_ve, VE[("vhat", 15, 3)])
            for ft in range(16):
                te.wait_ge(s_v[ft % 2], W_WOC(ft))
                if ft < 2:
                    for tc in range(4):
                        te.wait_ge(s_ve, VE[("vhat", 14 + ft, tc)])
                else:
                    for tc in range(4):
                        te.wait_ge(s_ve, VE[("woev", ft - 2, tc)])
                last = {}
                for ee in range(16):
                    for tc in range(4):
                        last[tc] = te.matmul(
                            banks[tc + 4 * (ft % 2)][:, :],
                            wkvb[:, (2 + ft % 2) * D + ee * 128:(2 + ft % 2) * D + (ee + 1) * 128],
                            wslot(vslot(ee), tc),
                            start=(ee == 0), stop=(ee == 15))
                for tc in range(4):
                    minc(last[tc], MM_WO(ft, tc))
            if phases < 2:
                return
            for s, sw in enumerate(SWEEPS):
                for i, ot in enumerate(sw):
                    te.wait_ge(s_g[ot % 2], W_G(ot))
                    te.wait_ge(s_u[ot % 2], W_U(ot))
                    if s == 0 and i == 0:
                        te.wait_ge(s_ve, VE[("woev", 15, 3)])
                    elif i == 0:
                        te.wait_ge(s_ve, VE[("dev", s - 1, 15, 3)])
                    else:
                        te.wait_ge(s_ac, AC[("silu", ot - 1, 1)])
                        te.wait_ge(s_ve, VE[("hm", ot - 1, 1)])
                    lastg, lastu = {}, {}
                    for dd in range(16):
                        for k in range(2):
                            lastg[k] = te.matmul(
                                banks[k][:, :],
                                gbuf[ot % 2][:, dd * 128:(dd + 1) * 128],
                                xn[:, dd * T + k * 512:dd * T + (k + 1) * 512],
                                start=(dd == 0), stop=(dd == 15))
                        for k in range(2):
                            lastu[k] = te.matmul(
                                banks[2 + k][:, :],
                                ubuf[ot % 2][:, dd * 128:(dd + 1) * 128],
                                xn[:, dd * T + k * 512:dd * T + (k + 1) * 512],
                                start=(dd == 0), stop=(dd == 15))
                    minc(lastg[1], MM_FFN[("g1", ot)])
                    minc(lastu[1], MM_FFN[("u1", ot)])
                    if i > 0:
                        te.wait_ge(s_ac, AC[("silu", ot - 1, 2)])
                        te.wait_ge(s_ve, VE[("hm", ot - 1, 2)])
                    for dd in range(16):
                        for k in range(2):
                            lastg[k] = te.matmul(
                                banks[4 + k][:, :],
                                gbuf[ot % 2][:, dd * 128:(dd + 1) * 128],
                                xn[:, dd * T + (2 + k) * 512:dd * T + (3 + k) * 512],
                                start=(dd == 0), stop=(dd == 15))
                        for k in range(2):
                            lastu[k] = te.matmul(
                                banks[6 + k][:, :],
                                ubuf[ot % 2][:, dd * 128:(dd + 1) * 128],
                                xn[:, dd * T + (2 + k) * 512:dd * T + (3 + k) * 512],
                                start=(dd == 0), stop=(dd == 15))
                    minc(lastg[1], MM_FFN[("g2", ot)])
                    minc(lastu[1], MM_FFN[("u2", ot)])
                nsw = len(sw)
                te.wait_ge(s_ve, VE[("hm", sw[-1], 2)])
                for ft in range(16):
                    te.wait_ge(s_d[ft % 2], W_D(s, ft))
                    if ft == 0:
                        te.wait_ge(s_ac, AC[("silu", sw[-1], 1)])
                    elif ft == 1:
                        te.wait_ge(s_ac, AC[("silu", sw[-1], 2)])
                    else:
                        for tc in range(4):
                            te.wait_ge(s_ve, VE[("dev", s, ft - 2, tc)])
                    last = {}
                    for iti in range(nsw):
                        for tc in range(4):
                            last[tc] = te.matmul(
                                banks[tc + 4 * (ft % 2)][:, :],
                                dbuf[ft % 2][:, iti * 128:(iti + 1) * 128],
                                wslot(iti, tc),
                                start=(iti == 0), stop=(iti == nsw - 1))
                    for tc in range(4):
                        minc(last[tc], MM_FFN[("d", s, ft, tc)])

    return nc


def _blk(W):
    return np.ascontiguousarray(
        W.reshape(16, 128, 16, 128).transpose(0, 3, 2, 1).reshape(16, 128, D))


def kernel(L0_attn_x, L0_attn_Wq, L0_attn_Wk, L0_attn_Wv, L0_attn_Wo,
           L0_ffn_lut, L0_ffn_gate_idx, L0_ffn_up_idx, L0_ffn_down_idx):
    x = np.asarray(L0_attn_x, dtype=np.float32)
    Wq = np.asarray(L0_attn_Wq, dtype=np.float32)
    Wk = np.asarray(L0_attn_Wk, dtype=np.float32)
    Wv = np.asarray(L0_attn_Wv, dtype=np.float32)
    Wo = np.asarray(L0_attn_Wo, dtype=np.float32)
    lut = np.asarray(L0_ffn_lut, dtype=np.float32)
    gidx = np.asarray(L0_ffn_gate_idx)
    uidx = np.asarray(L0_ffn_up_idx)
    didx = np.asarray(L0_ffn_down_idx)

    wqb, wkb, wvb, wob = _blk(Wq), _blk(Wk), _blk(Wv), _blk(Wo)
    lutb = np.ascontiguousarray(np.broadcast_to(lut[None, :], (128, 16)))

    g_can = gidx.reshape(43, 128, 16, 128).transpose(0, 3, 2, 1).reshape(43, 128, 2048)
    u_can = uidx.reshape(43, 128, 16, 128).transpose(0, 3, 2, 1).reshape(43, 128, 2048)
    d_can = didx.reshape(16, 128, 43, 128).transpose(0, 3, 2, 1).reshape(16, 128, 5504)

    in_maps = []
    for c in range(NC):
        bb, jj = c // 4, c % 4
        xs = x[bb, jj * T:(jj + 1) * T, :]
        xtm = np.ascontiguousarray(xs.reshape(128, 16, D).transpose(1, 0, 2))
        xft = np.ascontiguousarray(xs.T.reshape(16, 128, T))
        reg = np.zeros((128, REG_COLS), np.float32)
        a0, b0 = G_OWN[c]
        for s, ot in enumerate(range(a0, b0)):
            reg[:, s * 4096:s * 4096 + 2048] = g_can[ot]
            reg[:, s * 4096 + 2048:(s + 1) * 4096] = u_can[ot]
        for s, ft in enumerate((2 * c, 2 * c + 1)):
            reg[:, REG_D0 + s * 5504:REG_D0 + (s + 1) * 5504] = d_can[ft]
        regb = reg.astype(ml_dtypes.bfloat16)
        mask = np.zeros((128, 8), np.float32)
        for r in range(NC):
            if r // 4 == bb and (r % 4) < jj:
                mask[:, r] = 1.0
        in_maps.append({
            "xtm_h": xtm, "xft_h": xft,
            "wq_h": wqb, "wk_h": wkb, "wv_h": wvb, "wo_h": wob,
            "idx_h": regb, "lutb_h": lutb, "mask_h": mask,
        })

    nc = _build_nc()
    res = run_bass_kernel_spmd(nc, in_maps, core_ids=list(range(NC)))
    out = np.empty((2, 8192, D), np.float32)
    for c in range(NC):
        bb, jj = c // 4, c % 4
        yf = np.asarray(res.results[c]["y_h"]).reshape(D, T)
        out[bb, jj * T:(jj + 1) * T, :] = yf.T
    return out
